# version 16
# baseline (speedup 1.0000x reference)
from contextlib import ExitStack

import numpy as np
import ml_dtypes

import concourse.bass as bass
import concourse.mybir as mybir
from concourse.bass_utils import run_bass_kernel_spmd

F32 = mybir.dt.float32
BF16 = mybir.dt.bfloat16
AF = mybir.ActivationFunctionType
ALU = mybir.AluOpType

D = 2048
CD = 1024
UP = 1024
NH = 16
DH = 128
DR = 64
NKV = 4
KVD = NKV * DH
INTER = 5632
NIC = INTER // 128
ROPE_SCALE = 40.0
EPS = 1e-5
ISQ = float(1.0 / np.sqrt(128.0))

ENGINES = ("t", "v", "a", "g", "s")
ENG_ATTR = {"t": "tensor", "v": "vector", "a": "scalar", "g": "gpsimd", "s": "sync"}


class Buf:
    __slots__ = ("h", "w", "readers", "name")

    def __init__(self, h, name):
        self.h = h
        self.name = name
        self.w = []
        self.readers = []

    def __getitem__(self, key):
        return self.h[key]


class MK:
    def __init__(self, nc, es):
        self.nc = nc
        self.es = es
        self.ops = {e: [] for e in ENGINES}
        self.esem = {e: es.enter_context(nc.semaphore(f"es_{e}")) for e in ENGINES}
        self.inc_count = {e: 0 for e in ENGINES}
        self.inc_prefix = {e: [] for e in ENGINES}
        self.dma_sems = {}
        self._nsem = 0

    def sbuf(self, name, shape, dtype):
        b = Buf(self.es.enter_context(self.nc.sbuf_tensor(name, list(shape), dtype)), name)
        self.aliases = getattr(self, "aliases", {})
        self.aliases.setdefault(id(b.h), []).append(b)
        return b

    def sub(self, parent, name):
        fence = []
        for b in self.aliases.get(id(parent.h), []):
            fence.extend(b.w)
            fence.extend(b.readers)
        nb = Buf(parent.h, name)
        nb.w = fence
        self.aliases[id(parent.h)].append(nb)
        return nb

    def merge(self, parent, sub):
        parent.w = list(parent.w) + list(sub.w)
        parent.readers = list(parent.readers) + list(sub.readers)

    def psum(self, name, shape, dtype=F32):
        return Buf(self.es.enter_context(self.nc.psum_tensor(name, list(shape), dtype)), name)

    def dram(self, name, shape, dtype, kind=None):
        if kind is None:
            h = self.nc.dram_tensor(name, list(shape), dtype)
        else:
            h = self.nc.dram_tensor(name, list(shape), dtype, kind=kind)
        return Buf(h, name)

    def dram_slabbed(self, name, shape, dtype, nslab, kind=None):
        b = self.dram(name, shape, dtype, kind)
        subs = [Buf(b.h, f"{name}_s{i}") for i in range(nslab)]
        return b.h, subs

    def _slot(self, key):
        if key not in self.dma_sems:
            self._nsem += 1
            h = self.es.enter_context(self.nc.semaphore(f"dm{self._nsem}"))
            self.dma_sems[key] = [h, 0]
        return self.dma_sems[key]

    def op(self, eng, fn, reads=(), writes=(), writes_acc=(), acc_final=(),
           deps=(), inc=True):
        dl = list(deps)
        for b in reads:
            dl.extend(b.w)
        for b in writes:
            dl.extend(b.w)
            dl.extend(b.readers)
        idx = len(self.ops[eng])
        self.ops[eng].append((dl, fn, inc))
        self.inc_count[eng] += 1 if inc else 0
        self.inc_prefix[eng].append(self.inc_count[eng])
        tok = ("e", eng, idx)
        for b in reads:
            b.readers.append(tok)
        for b in writes:
            b.w = [tok]
            b.readers = []
        for b in writes_acc:
            b.w.append(tok)
        for b in acc_final:
            b.w = [tok]
            b.readers = []
        return tok

    def dma(self, eng, out, in_, reads=(), writes=(), writes_acc=(), slot=None,
            transpose=False, deps=(), allow_slow=False):
        if slot is None:
            lst = list(writes) + list(writes_acc)
            assert len(lst) == 1
            slot = lst[0]
        key = slot if isinstance(slot, str) else id(slot)
        sem = self._slot(key)
        dl = list(deps)
        if isinstance(slot, str) and sem[1] > 0:
            dl.append(("d", key, sem[1]))
        for b in reads:
            dl.extend(b.w)
        for b in writes:
            dl.extend(b.w)
            dl.extend(b.readers)
        sem[1] += 16
        ctok = ("d", key, sem[1])

        def fn(e, out=out, in_=in_, semh=sem[0], transpose=transpose,
               allow_slow=allow_slow):
            if transpose:
                ins = e.dma_start_transpose(out, in_)
            elif allow_slow:
                ins = e.dma_start(out=out, in_=in_, allow_slow_non_contiguous=True)
            else:
                ins = e.dma_start(out=out, in_=in_)
            ins.then_inc(semh, 16)
            return ins

        self.ops[eng].append((dl, fn, False))
        self.inc_prefix[eng].append(self.inc_count[eng])
        for b in reads:
            b.readers.append(ctok)
        for b in writes:
            b.w = [ctok]
            b.readers = []
        for b in writes_acc:
            b.w.append(ctok)
        return ctok

    def wait(self, eng, toks):
        self.ops[eng].append((list(toks), lambda e: None, False))
        self.inc_prefix[eng].append(self.inc_count[eng])

    def _resolve(self, tok):
        if tok[0] == "e":
            _, eng, idx = tok
            pre = self.inc_prefix[eng]
            v = pre[idx]
            if v == (pre[idx - 1] if idx > 0 else 0):
                assert v < self.inc_count[eng], \
                    f"dep on no-inc op with no later inc ({eng} idx {idx})"
                v = v + 1
            return (self.esem[eng], v)
        _, key, v = tok
        return (self.dma_sems[key][0], v)

    def build(self, block):
        for eng_name in ENGINES:
            ops = self.ops[eng_name]
            if not ops:
                continue

            def make(eng_name=eng_name, ops=ops):
                esem = self.esem[eng_name]

                def body(e):
                    watermark = {}
                    for deps, fn, inc in ops:
                        tgt = {}
                        for tok in deps:
                            semh, v = self._resolve(tok)
                            k = id(semh)
                            if v > tgt.get(k, (None, -1))[1]:
                                tgt[k] = (semh, v)
                        for semh, v in tgt.values():
                            if v > watermark.get(id(semh), 0):
                                e.wait_ge(semh, v)
                                watermark[id(semh)] = v
                        ins = fn(e)
                        if inc:
                            ins.then_inc(esem, 1)
                return body

            getattr(block, ENG_ATTR[eng_name])(make())

    def stats(self):
        return {e: len(self.ops[e]) for e in ENGINES}


def make_consts(S):
    bfd = ml_dtypes.bfloat16
    half = DR // 2
    inv_freq = 1.0 / 10000.0 ** (np.arange(0, half, 2, dtype=np.float32) / half)
    t = np.arange(S, dtype=np.float32) / ROPE_SCALE
    freqs = np.einsum("i,j->ij", t, inv_freq)
    rot = np.concatenate([freqs, freqs], axis=1)
    cos = np.cos(rot)
    sin = np.sin(rot)
    C1 = np.ones((S, DR), np.float32)
    B1 = np.zeros((S, DR), np.float32)
    C1[:, 0:32] = cos
    B1[:, 0:16] = -sin[:, 0:16]
    B1[:, 16:32] = sin[:, 16:32]
    CT = np.tile(C1, (1, NH)).astype(bfd)
    BT = np.tile(B1, (1, NH)).astype(bfd)
    CF = np.concatenate([C1.T, C1.T], axis=0).astype(bfd)
    BFm = np.concatenate([B1.T, B1.T], axis=0).astype(bfd)
    P128 = np.zeros((128, 128), np.float32)
    for blk in range(2):
        b0 = blk * 64
        for j in range(16):
            P128[b0 + 16 + j, b0 + j] = 1.0
            P128[b0 + j, b0 + 16 + j] = 1.0
    masks = np.zeros((128, 4 * 512), np.float32)
    jj = np.arange(128)[:, None]
    ii = np.arange(512)[None, :]
    for k in range(4):
        masks[:, k * 512:(k + 1) * 512] = (ii >= jj + 128 * k)
    return {
        "rope_ct": CT, "rope_bt": BT, "rope_cf": CF, "rope_bf": BFm,
        "perm": P128.astype(bfd), "ident": np.eye(128, dtype=bfd),
        "identf": np.eye(128, dtype=np.float32),
        "ones_row": np.ones((1, 128), bfd), "ones_col": np.ones((128, 1), bfd),
        "ones4": np.ones((4, 128), bfd),
        "ones_rowf": np.ones((1, 128), np.float32),
        "masks": masks.astype(bfd),
    }


def const_specs(S):
    return [
        ("rope_ct", [S, NH * DR], BF16, False),
        ("rope_bt", [S, NH * DR], BF16, False),
        ("rope_cf", [128, S], BF16, False),
        ("rope_bf", [128, S], BF16, False),
        ("perm", [128, 128], BF16, True),
        ("ident", [128, 128], BF16, False),
        ("identf", [128, 128], F32, True),
        ("ones_row", [1, 128], BF16, True),
        ("ones_col", [128, 1], BF16, True),
        ("ones4", [4, 128], BF16, True),
        ("ones_rowf", [1, 128], F32, True),
        ("masks", [128, 4 * 512], BF16, False),
    ]


WR_SPECS = {
    "attn_a_W_dkv": (128, 16384), "attn_a_W_dq": (128, 16384),
    "attn_a_W_uk": (128, 8192), "attn_a_W_uq": (128, 8192),
    "attn_a_W_qr": (128, 8192), "attn_a_W_uv": (128, 16384),
    "attn_a_W_kr": (128, 16384), "attn_a_W_out": (128, 32768),
    "ffn_gate_w": (1408, 8192), "ffn_up_w": (1408, 8192),
    "ffn_down_w": (2048, 5632),
    "attn_b_Wq": (512, 8192), "attn_b_Wk": (128, 8192),
    "attn_b_Wv": (128, 8192), "attn_b_Wo": (512, 8192),
}


def rearrange_weights(base):
    out = {}
    g = lambda n: base[n]
    out["attn_a_W_dkv"] = g("attn_a_W_dkv").reshape(16, 128, 1024).transpose(1, 0, 2)
    out["attn_a_W_dq"] = g("attn_a_W_dq").reshape(16, 128, 1024).transpose(1, 0, 2)
    out["attn_a_W_uk"] = g("attn_a_W_uk").reshape(8, 128, 1024).transpose(1, 0, 2)
    out["attn_a_W_uq"] = g("attn_a_W_uq").reshape(8, 128, 1024).transpose(1, 0, 2)
    out["attn_a_W_qr"] = g("attn_a_W_qr").reshape(8, 128, 1024).transpose(1, 0, 2)
    out["attn_a_W_uv"] = g("attn_a_W_uv").reshape(8, 128, 2, 1024).transpose(1, 2, 0, 3)
    out["attn_a_W_kr"] = g("attn_a_W_kr").reshape(16, 128, 1024).transpose(1, 0, 2)
    out["attn_a_W_out"] = g("attn_a_W_out").reshape(16, 128, 2048).transpose(1, 0, 2)
    out["ffn_gate_w"] = g("ffn_gate_w").T.reshape(16, 128, 11, 512).transpose(2, 1, 0, 3)
    out["ffn_up_w"] = g("ffn_up_w").T.reshape(16, 128, 11, 512).transpose(2, 1, 0, 3)
    out["ffn_down_w"] = g("ffn_down_w").T.reshape(4, 11, 128, 4, 512).transpose(0, 3, 2, 1, 4)
    out["attn_b_Wq"] = g("attn_b_Wq").T.reshape(16, 128, 4, 512).transpose(2, 1, 0, 3)
    out["attn_b_Wk"] = g("attn_b_Wk").T.reshape(16, 128, 512).transpose(1, 0, 2)
    out["attn_b_Wv"] = g("attn_b_Wv").T.reshape(16, 128, 512).transpose(1, 0, 2)
    out["attn_b_Wo"] = g("attn_b_Wo").T.reshape(16, 128, 4, 512).transpose(2, 1, 0, 3)
    return {k: np.ascontiguousarray(v.reshape(WR_SPECS[k])) for k, v in out.items()}

WEIGHT_SPECS = [
    ("attn_a_W_dkv", [D, CD]), ("attn_a_W_dq", [D, CD]), ("attn_a_W_uk", [CD, UP]),
    ("attn_a_W_uv", [CD, NH * DH]), ("attn_a_W_uq", [CD, UP]),
    ("attn_a_W_kr", [D, NH * DR]), ("attn_a_W_qr", [UP, NH * DR]),
    ("attn_a_W_out", [NH * DH, D]),
    ("attn_b_Wq", [NH * DH, D]), ("attn_b_Wk", [KVD, D]), ("attn_b_Wv", [KVD, D]),
    ("attn_b_Wo", [D, NH * DH]),
    ("ffn_gate_w", [INTER, D]), ("ffn_up_w", [INTER, D]), ("ffn_down_w", [D, INTER]),
]


def build_program(S, sublayers=(1, 2, 3, 4)):
    assert S % 512 == 0
    NT = S // 128
    NS = S // 512

    nc = bass.Bass("TRN2", target_bir_lowering=False)
    es = ExitStack()
    mk = MK(nc, es)

    xin_h, XIN = mk.dram_slabbed("x", [S, D], F32, NS, kind="ExternalInput")
    win_r = {nm: mk.dram(nm + "_R", list(shape), F32, kind="ExternalInput")
             for nm, shape in WR_SPECS.items()}
    bo_in = mk.dram("attn_b_bo", [1, D], F32, kind="ExternalInput")
    gs_in = mk.dram("ffn_gate_s2d", [NIC, 128], F32, kind="ExternalInput")
    us_in = mk.dram("ffn_up_s2d", [NIC, 128], F32, kind="ExternalInput")
    cst = {nm: mk.dram(nm, shape, dt, kind="ExternalInput")
           for nm, shape, dt, _ in const_specs(S)}
    ds_row = mk.dram("ffn_down_s_row", [1, D], F32, kind="ExternalInput")
    xout_h, XOUT = mk.dram_slabbed("out", [S, D], F32, NS, kind="ExternalOutput")

    resa_h, RESA = mk.dram_slabbed("res_a", [S, D], F32, NS)
    resb_h, RESB = mk.dram_slabbed("res_b", [S, D], F32, NS)
    lnh_h, LNHS = mk.dram_slabbed("lnh", [S, D], BF16, NS)
    CKV = mk.dram("ckv", [CD, S], BF16)
    QFM = mk.dram("qfm", [NH * DH, S], BF16)
    MRW = mk.dram("mrw", [NH, 512], BF16)
    WR = {nm: mk.dram("wr_" + nm, list(WR_SPECS[nm]), BF16) for nm in WR_SPECS}

    sb_ct = {nm: mk.sbuf("c_" + nm, shape, dt)
             for nm, shape, dt, keep in const_specs(S) if keep}
    GS = mk.sbuf("gs", [128, NIC], F32)
    US = mk.sbuf("us", [128, NIC], F32)
    SHARED4 = mk.sbuf("shared4", [128, 2048], BF16)
    BO = mk.sbuf("bo", [1, D], BF16)
    XT = [mk.sbuf(f"xt{i}", [128, D], F32) for i in range(2)]
    LNT = [mk.sbuf("lnt0", [128, D], BF16)] * 2
    LNF = [mk.sbuf(f"lnf{i}", [128, 16 * 512], BF16) for i in range(2)]
    WS = [mk.sbuf(f"ws{i}", [128, 8192], BF16) for i in range(4)]
    BIG = [mk.sbuf(f"big{i}", [128, 16384], BF16) for i in range(2)]
    MED = [mk.sbuf(f"med{i}", [128, 2048], BF16) for i in range(2)]
    SMF = [mk.sbuf(f"smf{i}", [128, 512], F32) for i in range(4)]
    MSEL = mk.sbuf("msel", [1, 512], BF16)
    BN24 = [mk.sbuf(f"bn24_{i}", [128, 24], F32) for i in range(2)]
    ST1 = [mk.sbuf(f"st1_{i}", [128, 8], F32) for i in range(2)]
    CM1 = mk.sbuf("cm1", [128, 1], F32)
    CEPS = mk.sbuf("ceps", [128, 1], F32)
    PS = [mk.psum(f"ps{i}", [128, 512]) for i in range(4)]
    PM = [mk.psum(f"pm{i}", [128, 512]) for i in range(4)]

    conv_done = {}
    nconsts = sum(1 for *_, keep in const_specs(S) if keep)

    for nm in sb_ct:
        mk.dma("s", sb_ct[nm].h[:, :], cst[nm].h[:, :], reads=[cst[nm]],
               writes=[sb_ct[nm]], slot="consts")
    consts_tok = ("d", "consts", 16 * nconsts)
    mk.op("v", lambda v: v.memset(CM1.h[:, :], -20.0), writes=[CM1])
    mk.op("v", lambda v: v.memset(CEPS.h[:, :], EPS), writes=[CEPS])

    conv_order = ["attn_a_W_dkv", "attn_a_W_dq", "attn_a_W_uq", "attn_a_W_qr",
                  "attn_a_W_uv", "attn_a_W_uk", "attn_a_W_kr", "attn_a_W_out",
                  "ffn_gate_w", "ffn_up_w", "ffn_down_w",
                  "attn_b_Wq", "attn_b_Wk", "attn_b_Wv", "attn_b_Wo"]
    conv_pieces = []
    pieces_left = {}
    for nm in conv_order:
        rows, cols = WR_SPECS[nm]
        step = max(1, min(rows, (1 << 22) // (cols * 4)))
        for r0 in range(0, rows, step):
            conv_pieces.append((nm, r0, min(r0 + step, rows)))
            pieces_left[nm] = pieces_left.get(nm, 0) + 1
    cctr = [0]

    def drain_conv(n=1 << 30):
        k = 0
        while conv_pieces and k < n:
            nm, r0, r1 = conv_pieces.pop(0)
            tk = mk.dma("g", WR[nm].h[r0:r1, :], win_r[nm].h[r0:r1, :],
                        reads=[win_r[nm]], writes_acc=[WR[nm]],
                        slot=f"conv{cctr[0] % 2}")
            cctr[0] += 1
            pieces_left[nm] -= 1
            if pieces_left[nm] == 0:
                conv_done[nm] = tk
            k += 1

    drain_conv(sum(1 for nm, *_ in conv_pieces if nm.startswith("attn_a")))

    for src, dstbuf, n in ((gs_in, GS, NIC), (us_in, US, NIC)):
        tmp = SMF[0]
        mk.dma("s", tmp.h[0:n, 0:128], src.h[:, :], reads=[src], writes=[tmp])
        pt = PS[0]
        mk.op("t", lambda t, tmp=tmp, pt=pt, n=n: t.transpose(
            pt.h[:, 0:n], tmp.h[0:n, 0:128], sb_ct["identf"].h[0:n, 0:n]),
            reads=[tmp, sb_ct["identf"]], writes=[pt], deps=[consts_tok])
        mk.op("v", lambda v, pt=pt, dstbuf=dstbuf, n=n: v.tensor_copy(
            dstbuf.h[:, 0:n], pt.h[:, 0:n]), reads=[pt], writes=[dstbuf])

    mk.dma("g", BO.h[:, :], bo_in.h[:, :], reads=[bo_in], writes=[BO], slot="conv")

    def emit_ln_scan(SRC, hook=None):
        lnth = [mk.sub(LNT[0], "lnt_h0"), mk.sub(LNT[0], "lnt_h1")]
        for i in range(NT):
            sl = i // 4
            xt = XT[i % 2]
            bn = BN24[i % 2]
            st = ST1[i % 2]
            mk.dma("s", xt.h[:, :], lnh_src_h(SRC)[i * 128:(i + 1) * 128, :],
                   reads=[SRC[sl]], writes=[xt])
            for g in range(4):
                mk.op("v", lambda v, g=g, xt=xt, bn=bn: v.bn_stats(
                    bn.h[:, 6 * g:6 * g + 6], xt.h[:, 512 * g:512 * (g + 1)]),
                    reads=[xt], writes=[bn] if g == 0 else (),
                    writes_acc=[bn] if g > 0 else (), inc=(g == 3))
            mk.op("v", lambda v, bn=bn, st=st: v.bn_aggr(st.h[:, 0:2], bn.h[:, :]),
                  reads=[bn], writes=[st])
            mk.op("a", lambda a, st=st: a.activation(
                st.h[:, 3:4], st.h[:, 1:2], AF.Sqrt, bias=CEPS.h[:, 0:1]),
                reads=[st, CEPS], writes_acc=[st])
            mk.op("v", lambda v, st=st: v.reciprocal(
                st.h[:, 4:5], st.h[:, 3:4]), reads=[st], writes_acc=[st])
            mk.op("v", lambda v, st=st: v.tensor_scalar(
                st.h[:, 5:6], st.h[:, 0:1], st.h[:, 4:5], -1.0, ALU.mult, ALU.mult),
                reads=[st], writes_acc=[st])
            for hf in range(2):
                lt = lnth[hf]
                mk.op("a", lambda a, xt=xt, lt=lt, hf=hf, st=st: a.activation(
                    lt.h[:, hf * 1024:(hf + 1) * 1024],
                    xt.h[:, hf * 1024:(hf + 1) * 1024], AF.Identity,
                    bias=st.h[:, 5:6], scale=st.h[:, 4:5]),
                    reads=[xt, st], writes=[lt])
                mk.dma("g", lnh_h[i * 128:(i + 1) * 128, hf * 1024:(hf + 1) * 1024],
                       lt.h[:, hf * 1024:(hf + 1) * 1024], reads=[lt],
                       writes=[LNHS[sl]] if (i % 4 == 0 and hf == 0) else (),
                       writes_acc=[LNHS[sl]] if not (i % 4 == 0 and hf == 0) else (),
                       slot=f"lns{(2 * i + hf) % 4}")
            if i % 4 == 3 and hook is not None:
                hook(sl)

    def lnh_src_h(SRC):
        return SRC[0].h

    def lnf_prefetch_hook(pre):
        def hook(sl):
            if sl < 2:
                load_lnf(LNF[sl % 2], sl)
                pre.add(sl)
        return hook

    def load_lnf(dst, s):
        toks = []
        for c in range(16):
            toks.append(mk.dma("a", dst.h[:, c * 512:(c + 1) * 512],
                               lnh_h[s * 512:(s + 1) * 512, c * 128:(c + 1) * 128],
                               reads=[LNHS[s]],
                               writes=[dst] if c == 0 else (),
                               slot=dst, transpose=True))
        dst.w = [toks[-1]]
        return toks[-1]

    def emit_ffn(SRC, DST, dst_h):
        drain_conv()
        pre = set()
        emit_ln_scan(SRC, lnf_prefetch_hook(pre))
        dsbc = mk.sub(SHARED4, "dsbc")
        for q in range(4):
            mk.dma("s", SMF[0].h[0:1, 0:512], ds_row.h[0:1, q * 512:(q + 1) * 512],
                   reads=[ds_row], writes=[SMF[0]])
            pq = PS[0]
            mk.op("t", lambda t, pq=pq: t.matmul(
                pq.h[:, :], sb_ct["ones_rowf"].h[:, :], SMF[0].h[0:1, 0:512],
                start=True, stop=True), reads=[SMF[0], sb_ct["ones_rowf"]], writes=[pq])
            mk.op("v", lambda v, pq=pq, q=q: v.tensor_copy(
                dsbc.h[:, q * 512:(q + 1) * 512], pq.h[:, :]),
                reads=[pq], writes=[dsbc])
        hdn = [(BIG[0], c) if c < 32 else (BIG[1], c - 32) for c in range(NIC)]
        for s in range(NS):
            lnf = LNF[s % 2]
            if s not in pre:
                load_lnf(lnf, s)
            for grp in range(11):
                wg = WS[2 * (grp % 2)]
                wu = WS[2 * (grp % 2) + 1]
                ic0 = grp * 4
                for wbuf, nm in ((wg, "ffn_gate_w"), (wu, "ffn_up_w")):
                    mk.dma("s", wbuf.h[:, 0:8192],
                           WR[nm].h[grp * 128:(grp + 1) * 128, :],
                           reads=[WR[nm]], writes=[wbuf],
                           deps=[conv_done[nm]])
                for ici in range(4):
                    ic = ic0 + ici
                    pg = PS[2 * (ici % 2)]
                    pu = PS[2 * (ici % 2) + 1]
                    for pbuf, wbuf in ((pg, wg), (pu, wu)):
                        for hc in range(16):
                            mk.op("t", lambda t, wbuf=wbuf, pbuf=pbuf, lnf=lnf, hc=hc,
                                  ici=ici: t.matmul(
                                      pbuf.h[:, :],
                                      wbuf.h[:, hc * 512 + ici * 128: hc * 512 + (ici + 1) * 128],
                                      lnf.h[:, hc * 512:(hc + 1) * 512],
                                      start=(hc == 0), stop=(hc == 15)),
                                  reads=[wbuf, lnf],
                                  writes=[pbuf] if hc == 0 else (),
                                  acc_final=[pbuf] if hc == 15 else (),
                                  inc=(hc == 15))
                    sil = MED[ic % 2]
                    mk.op("a", lambda a, pg=pg, sil=sil, ic=ic: a.activation(
                        sil.h[:, 0:512], pg.h[:, :], AF.Silu, scale=GS.h[:, ic:ic + 1]),
                        reads=[pg, GS], writes=[sil])
                    hb, hci = hdn[ic]
                    mk.op("v", lambda v, pu=pu, sil=sil, hb=hb, hci=hci, ic=ic:
                          v.scalar_tensor_tensor(
                              hb.h[:, hci * 512:(hci + 1) * 512], pu.h[:, :],
                              US.h[:, ic:ic + 1], sil.h[:, 0:512], ALU.mult, ALU.mult),
                          reads=[pu, sil, US],
                          writes=[hb] if ic in (0, 32) else (),
                          writes_acc=[hb] if ic not in (0, 32) else ())
            for hs in range(4):
                for icg in range(4):
                    wd = WS[icg % 2]
                    mk.dma("s", wd.h[:, 0:5632],
                           WR["ffn_down_w"].h[(icg * 4 + hs) * 128:
                                              (icg * 4 + hs + 1) * 128, :],
                           reads=[WR["ffn_down_w"]], writes=[wd],
                           deps=[conv_done["ffn_down_w"]])
                    for tsub in range(4):
                        po = PM[tsub]
                        for icl in range(11):
                            ic = icg * 11 + icl
                            hb, hci = hdn[ic]
                            first = (icg == 0 and icl == 0)
                            last = (icg == 3 and icl == 10)
                            mk.op("t", lambda t, po=po, wd=wd, hb=hb, hci=hci,
                                  icl=icl, tsub=tsub, first=first, last=last: t.matmul(
                                      po.h[:, :],
                                      hb.h[:, hci * 512 + tsub * 128: hci * 512 + (tsub + 1) * 128],
                                      wd.h[:, icl * 512:(icl + 1) * 512],
                                      start=first, stop=last),
                                  reads=[wd, hb],
                                  writes=[po] if first else (),
                                  acc_final=[po] if last else (),
                                  inc=(icl == 10))
                for tsub in range(4):
                    po = PM[tsub]
                    xs = SMF[tsub % 2]
                    xo = SMF[2 + tsub % 2]
                    trow = s * 512 + tsub * 128
                    mk.dma("s", xs.h[:, :],
                           lnh_src_h(SRC)[trow:trow + 128, hs * 512:(hs + 1) * 512],
                           reads=[SRC[s]], writes=[xs])
                    mk.op("v", lambda v, po=po, xo=xo, hs=hs: v.scalar_tensor_tensor(
                        xo.h[:, :], po.h[:, :], 1.0,
                        dsbc.h[:, hs * 512:(hs + 1) * 512], ALU.mult, ALU.mult),
                        reads=[po, dsbc], writes=[xo])
                    mk.op("v", lambda v, xs=xs, xo=xo: v.tensor_add(
                        xo.h[:, :], xo.h[:, :], xs.h[:, :]),
                        reads=[xs, xo], writes_acc=[xo])
                    mk.dma("g", dst_h[trow:trow + 128, hs * 512:(hs + 1) * 512],
                           xo.h[:, :], reads=[xo],
                           writes=[DST[s]] if (hs == 0 and tsub == 0) else (),
                           writes_acc=[DST[s]] if not (hs == 0 and tsub == 0) else (),
                           slot=f"st{(hs * 4 + tsub) % 4}")

    def emit_attn_a(SRC, DST, dst_h):
        pre = set()
        emit_ln_scan(SRC, lnf_prefetch_hook(pre))
        aCKV = mk.sub(BIG[0], "aCKV")
        aCQ = mk.sub(BIG[0], "aCQ")
        aQCF = mk.sub(BIG[0], "aQCF")
        aQCT = mk.sub(BIG[0], "aQCT")
        aQRT = mk.sub(BIG[1], "aQRT")
        aQRO = mk.sub(BIG[1], "aQRO")
        aVT = mk.sub(BIG[1], "aVT")
        O = {"aCKV": 0, "aCQ": 4096, "aQCF": 8192, "aQCT": 12288,
             "aQRT": 0, "aQRO": 4096, "aVT": 8192}

        QSH = mk.sub(LNT[0], "qsha")
        mk.op("v", lambda v: v.memset(QSH.h[:, 0:1024], 0.0), writes=[QSH])
        mtacc = mk.sub(XT[1], "mtacc")
        mk.op("v", lambda v: v.memset(mtacc.h[:, :], 0.0), writes=[mtacc])

        def acc_mm(pbuf, lhsT_ap, rhs_ap, kidx, klast, rbufs, extra=()):
            mk.op("t", lambda t, lhsT_ap=lhsT_ap, rhs_ap=rhs_ap, pbuf=pbuf,
                  kidx=kidx, klast=klast: t.matmul(
                      pbuf.h[:, :], lhsT_ap, rhs_ap,
                      start=(kidx == 0), stop=(kidx == klast)),
                  reads=rbufs, deps=extra,
                  writes=[pbuf] if kidx == 0 else (),
                  acc_final=[pbuf] if kidx == klast else (),
                  inc=(kidx == klast))

        for s in range(NS):
            lnf = LNF[s % 2]
            if s not in pre:
                load_lnf(lnf, s)
            for wi, (nm, hh) in enumerate((("attn_a_W_dkv", 0), ("attn_a_W_dkv", 1),
                                           ("attn_a_W_dq", 0), ("attn_a_W_dq", 1))):
                mk.dma("s", WS[wi].h[:, 0:8192],
                       WR[nm].h[:, hh * 8192:(hh + 1) * 8192],
                       reads=[WR[nm]], writes=[WS[wi]], deps=[conv_done[nm]])
            for dstb, w01, eng in ((aCKV, (WS[0], WS[1]), "v"), (aCQ, (WS[2], WS[3]), "a")):
                for cc in range(8):
                    pbuf = PS[cc % 4]
                    for dc in range(16):
                        wb = w01[dc // 8]
                        acc_mm(pbuf, wb.h[:, (dc % 8) * 1024 + cc * 128:
                                          (dc % 8) * 1024 + (cc + 1) * 128],
                               lnf.h[:, dc * 512:(dc + 1) * 512], dc, 15, [wb, lnf])
                    off = O[dstb.name] + cc * 512
                    mk.op(eng, lambda e, pbuf=pbuf, dstb=dstb, off=off, eng=eng: (
                        e.tensor_copy if eng == "v" else e.copy)(
                        dstb.h[:, off:off + 512], pbuf.h[:, :]),
                        reads=[pbuf], writes=[dstb])
            mk.dma("g", CKV.h[:, s * 512:(s + 1) * 512].rearrange(
                "(c p) o -> p c o", p=128),
                BIG[0].h[:, 0:4096].rearrange("p (c o) -> p c o", c=8),
                reads=[aCKV], writes_acc=[CKV], slot="ckv")
            mk.dma("s", WS[0].h[:, 0:8192], WR["attn_a_W_uq"].h[:, :],
                   reads=[WR["attn_a_W_uq"]], writes=[WS[0]],
                   deps=[conv_done["attn_a_W_uq"]])
            mk.dma("s", WS[1].h[:, 0:8192], WR["attn_a_W_qr"].h[:, :],
                   reads=[WR["attn_a_W_qr"]], writes=[WS[1]],
                   deps=[conv_done["attn_a_W_qr"]])
            for wi, hh in ((2, 0), (3, 1)):
                mk.dma("s", WS[wi].h[:, 0:8192],
                       WR["attn_a_W_uv"].h[:, hh * 8192:(hh + 1) * 8192],
                       reads=[WR["attn_a_W_uv"]], writes=[WS[wi]],
                       deps=[conv_done["attn_a_W_uv"]])
            for uc in range(8):
                pbuf = PS[uc % 4]
                for cc in range(8):
                    acc_mm(pbuf, WS[0].h[:, cc * 1024 + uc * 128: cc * 1024 + (uc + 1) * 128],
                           aCQ.h[:, 4096 + cc * 512: 4096 + (cc + 1) * 512], cc, 7,
                           [WS[0], aCQ])
                mk.op("v", lambda v, pbuf=pbuf, uc=uc: v.tensor_copy(
                    aQCF.h[:, 8192 + uc * 512: 8192 + (uc + 1) * 512], pbuf.h[:, :]),
                    reads=[pbuf], writes=[aQCF])
            for tt in range(4):
                for half in range(2):
                    pbuf = PS[(tt * 2 + half) % 4]
                    for cc in range(8):
                        acc_mm(pbuf, aCQ.h[:, 4096 + cc * 512 + tt * 128:
                                           4096 + cc * 512 + (tt + 1) * 128],
                               WS[0].h[:, cc * 1024 + half * 512: cc * 1024 + (half + 1) * 512],
                               cc, 7, [aCQ, WS[0]])
                    off = 12288 + tt * 1024 + half * 512
                    mk.op("a", lambda a, pbuf=pbuf, off=off: a.copy(
                        aQCT.h[:, off:off + 512], pbuf.h[:, :]),
                        reads=[pbuf], writes=[aQCT])
            for tt in range(4):
                for half in range(2):
                    pbuf = PS[(tt * 2 + half) % 4]
                    for uc in range(8):
                        acc_mm(pbuf, aQCF.h[:, 8192 + uc * 512 + tt * 128:
                                            8192 + uc * 512 + (tt + 1) * 128],
                               WS[1].h[:, uc * 1024 + half * 512: uc * 1024 + (half + 1) * 512],
                               uc, 7, [aQCF, WS[1]])
                    off = tt * 1024 + half * 512
                    mk.op("v", lambda v, pbuf=pbuf, off=off: v.tensor_copy(
                        aQRT.h[:, off:off + 512], pbuf.h[:, :]),
                        reads=[pbuf], writes=[aQRT])
            for tt in range(4):
                med = MED[tt % 2]
                trow = s * 512 + tt * 128
                mk.dma("s", med.h[:, 0:1024], cst["rope_ct"].h[trow:trow + 128, :],
                       reads=[cst["rope_ct"]], writes=[med], slot=f"ct{tt % 2}")
                mk.dma("s", med.h[:, 1024:2048], cst["rope_bt"].h[trow:trow + 128, :],
                       reads=[cst["rope_bt"]], writes_acc=[med], slot=f"ct{tt % 2}")
                qv = aQRT.h[:, tt * 1024:(tt + 1) * 1024].rearrange(
                    "p (b c) -> p b c", b=16)
                sv_ = QSH.h[:, 0:1024].rearrange("p (b c) -> p b c", b=16)
                mk.op("v", lambda v, qv=qv, sv_=sv_: v.tensor_copy(
                    sv_[:, :, 0:16], qv[:, :, 16:32]), reads=[aQRT], writes=[QSH])
                mk.op("v", lambda v, qv=qv, sv_=sv_: v.tensor_copy(
                    sv_[:, :, 16:32], qv[:, :, 0:16]), reads=[aQRT], writes_acc=[QSH])
                mk.op("v", lambda v, tt=tt, med=med: v.tensor_mul(
                    aQRO.h[:, 4096 + tt * 1024: 4096 + (tt + 1) * 1024],
                    aQRT.h[:, tt * 1024:(tt + 1) * 1024], med.h[:, 0:1024]),
                    reads=[aQRT, med], writes=[aQRO])
                mk.op("v", lambda v, med=med: v.tensor_mul(
                    QSH.h[:, 0:1024], QSH.h[:, 0:1024], med.h[:, 1024:2048]),
                    reads=[QSH, med], writes=[QSH])
                mk.op("v", lambda v, tt=tt: v.tensor_add(
                    aQRO.h[:, 4096 + tt * 1024: 4096 + (tt + 1) * 1024],
                    aQRO.h[:, 4096 + tt * 1024: 4096 + (tt + 1) * 1024],
                    QSH.h[:, 0:1024]),
                    reads=[aQRO, QSH], writes_acc=[aQRO])
            for tt in range(4):
                for os_ in range(4):
                    pbuf = PS[(tt * 4 + os_) % 4]
                    wb = WS[2 + os_ // 2]
                    for cc in range(8):
                        acc_mm(pbuf, aCKV.h[:, cc * 512 + tt * 128: cc * 512 + (tt + 1) * 128],
                               wb.h[:, cc * 1024 + (os_ % 2) * 512:
                                    cc * 1024 + (os_ % 2 + 1) * 512], cc, 7, [aCKV, wb])
                    off = 8192 + tt * 2048 + os_ * 512
                    mk.op("a", lambda a, pbuf=pbuf, off=off: a.copy(
                        aVT.h[:, off:off + 512], pbuf.h[:, :]),
                        reads=[pbuf], writes=[aVT])
            for tt in range(4):
                for h in range(NH):
                    pm = PM[h % 4]
                    vap = aVT.h[:, 8192 + tt * 2048 + h * 128: 8192 + tt * 2048 + (h + 1) * 128]
                    mk.op("t", lambda t, pm=pm, vap=vap, tt=tt, h=h: t.matmul(
                        pm.h[:, 0:64], vap,
                        aQCT.h[:, 12288 + tt * 1024 + h * 64: 12288 + tt * 1024 + h * 64 + 64],
                        start=True, stop=False),
                        reads=[aVT, aQCT], writes=[pm], inc=False)
                    mk.op("t", lambda t, pm=pm, vap=vap, tt=tt, h=h: t.matmul(
                        pm.h[:, 64:128], vap,
                        aQRO.h[:, 4096 + tt * 1024 + h * 64: 4096 + tt * 1024 + h * 64 + 64],
                        start=False, stop=True),
                        reads=[aVT, aQRO], acc_final=[pm], inc=True)
                    mk.op("v", lambda v, pm=pm, h=h: v.tensor_add(
                        mtacc.h[:, h * 128:(h + 1) * 128],
                        mtacc.h[:, h * 128:(h + 1) * 128], pm.h[:, 0:128]),
                        reads=[pm, mtacc], writes_acc=[mtacc])
            if s >= 1:
                drain_conv(6)

        mte = mk.sub(MED[0], "mte")
        for h in range(NH):
            mk.op("v", lambda v, h=h: v.tensor_scalar_mul(
                mte.h[:, h * 128:(h + 1) * 128],
                mtacc.h[:, h * 128:(h + 1) * 128], ISQ),
                reads=[mtacc], writes=[mte])
        g01 = (mk.sub(BIG[0], "g0"), mk.sub(BIG[1], "g1"))
        for h in range(NH):
            wsb = WS[h % 2]
            mk.dma("s", wsb.h[:, 0:2048],
                   WR["attn_a_W_out"].h[:, h * 2048:(h + 1) * 2048],
                   reads=[WR["attn_a_W_out"]], writes=[wsb],
                   deps=[conv_done["attn_a_W_out"]])
            gb = g01[h // 8]
            goff = (h % 8) * 2048
            for os_ in range(4):
                pbuf = PS[os_ % 4]
                mk.op("t", lambda t, pbuf=pbuf, h=h, os_=os_, wsb=wsb: t.matmul(
                    pbuf.h[:, :], mte.h[:, h * 128:(h + 1) * 128],
                    wsb.h[:, os_ * 512:(os_ + 1) * 512], start=True, stop=True),
                    reads=[mte, wsb], writes=[pbuf])
                mk.op("a", lambda a, pbuf=pbuf, gb=gb, goff=goff, os_=os_: a.copy(
                    gb.h[:, goff + os_ * 512: goff + (os_ + 1) * 512], pbuf.h[:, :]),
                    reads=[pbuf], writes=[gb] if (h % 8 == 0 and os_ == 0) else (),
                    writes_acc=[gb] if not (h % 8 == 0 and os_ == 0) else ())

        for s in range(NS):
            lnf = LNF[s % 2]
            load_lnf(lnf, s)
            med = MED[1]
            mk.dma("s", med.h[:, 0:512], cst["rope_cf"].h[:, s * 512:(s + 1) * 512],
                   reads=[cst["rope_cf"]], writes=[med], slot="cf")
            mk.dma("s", med.h[:, 512:1024], cst["rope_bf"].h[:, s * 512:(s + 1) * 512],
                   reads=[cst["rope_bf"]], writes_acc=[med], slot="cf")
            for wi, hh in ((1, 0), (2, 1)):
                mk.dma("s", WS[wi].h[:, 0:8192],
                       WR["attn_a_W_kr"].h[:, hh * 8192:(hh + 1) * 8192],
                       reads=[WR["attn_a_W_kr"]], writes=[WS[wi]],
                       deps=[conv_done["attn_a_W_kr"]])
            for rc in range(8):
                pbuf = PS[rc % 4]
                for dc in range(16):
                    wb = WS[1 + dc // 8]
                    acc_mm(pbuf, wb.h[:, (dc % 8) * 1024 + rc * 128:
                                      (dc % 8) * 1024 + (rc + 1) * 128],
                           lnf.h[:, dc * 512:(dc + 1) * 512], dc, 15, [wb, lnf])
                mk.op("v", lambda v, pbuf=pbuf, rc=rc: v.tensor_copy(
                    WS[3].h[:, rc * 512:(rc + 1) * 512], pbuf.h[:, :]),
                    reads=[pbuf], writes=[WS[3]] if rc == 0 else (),
                    writes_acc=[WS[3]] if rc != 0 else ())
            mk.dma("s", WS[0].h[:, 0:8192], WR["attn_a_W_uk"].h[:, :],
                   reads=[WR["attn_a_W_uk"]], writes=[WS[0]],
                   deps=[conv_done["attn_a_W_uk"]])
            mk.dma("s", WS[2].h[:, 0:4096].rearrange("p (c o) -> p c o", c=8),
                   CKV.h[:, s * 512:(s + 1) * 512].rearrange("(c p) o -> p c o", p=128),
                   reads=[CKV], writes=[WS[2]])
            for uc in range(8):
                pbuf = PS[uc % 4]
                for cc in range(8):
                    acc_mm(pbuf, WS[0].h[:, cc * 1024 + uc * 128: cc * 1024 + (uc + 1) * 128],
                           WS[2].h[:, cc * 512:(cc + 1) * 512], cc, 7, [WS[0], WS[2]])
                mk.op("v", lambda v, pbuf=pbuf, uc=uc, lnf=lnf: v.tensor_copy(
                    lnf.h[0:64, 2 * uc * 512:(2 * uc + 1) * 512], pbuf.h[0:64, :]),
                    reads=[pbuf], writes=[lnf] if uc == 0 else (),
                    writes_acc=[lnf] if uc != 0 else ())
                mk.op("v", lambda v, pbuf=pbuf, uc=uc, lnf=lnf: v.tensor_copy(
                    lnf.h[0:64, (2 * uc + 1) * 512:(2 * uc + 2) * 512], pbuf.h[64:128, :]),
                    reads=[pbuf], writes_acc=[lnf])
            for rc in range(8):
                pm0 = PM[rc % 2]
                mk.op("t", lambda t, pm0=pm0, rc=rc: t.matmul(
                    pm0.h[:, :], sb_ct["perm"].h[:, :],
                    WS[3].h[:, rc * 512:(rc + 1) * 512], start=True, stop=True),
                    reads=[sb_ct["perm"], WS[3]], writes=[pm0])
                mk.op("v", lambda v, rc=rc, med=med: v.tensor_mul(
                    SMF[0].h[:, :], WS[3].h[:, rc * 512:(rc + 1) * 512],
                    med.h[:, 0:512]), reads=[WS[3], med], writes=[SMF[0]])
                mk.op("v", lambda v, pm0=pm0, med=med: v.tensor_mul(
                    SMF[1].h[:, :], pm0.h[:, :], med.h[:, 512:1024]),
                    reads=[pm0, med], writes=[SMF[1]])
                mk.op("v", lambda v, rc=rc, lnf=lnf: v.tensor_add(
                    lnf.h[64:128, 2 * rc * 512:(2 * rc + 1) * 512],
                    SMF[0].h[0:64, :], SMF[1].h[0:64, :]),
                    reads=[SMF[0], SMF[1]], writes_acc=[lnf])
                mk.op("v", lambda v, rc=rc, lnf=lnf: v.tensor_add(
                    lnf.h[64:128, (2 * rc + 1) * 512:(2 * rc + 2) * 512],
                    SMF[0].h[64:128, :], SMF[1].h[64:128, :]),
                    reads=[SMF[0], SMF[1]], writes_acc=[lnf])
            for tt in range(4):
                for os_ in range(4):
                    pbuf = PS[os_]
                    for h in range(NH):
                        gb = g01[h // 8]
                        acc_mm(pbuf, lnf.h[:, h * 512 + tt * 128: h * 512 + (tt + 1) * 128],
                               gb.h[:, (h % 8) * 2048 + os_ * 512:
                                    (h % 8) * 2048 + (os_ + 1) * 512], h, 15, [lnf, gb])
                for os_ in range(4):
                    pbuf = PS[os_]
                    xs = SMF[os_ % 2]
                    xo = SMF[2 + os_ % 2]
                    trow = s * 512 + tt * 128
                    mk.dma("s", xs.h[:, :],
                           lnh_src_h(SRC)[trow:trow + 128, os_ * 512:(os_ + 1) * 512],
                           reads=[SRC[s]], writes=[xs])
                    mk.op("v", lambda v, pbuf=pbuf, xs=xs, xo=xo: v.tensor_add(
                        xo.h[:, :], xs.h[:, :], pbuf.h[:, :]),
                        reads=[pbuf, xs], writes=[xo])
                    mk.dma("g", dst_h[trow:trow + 128, os_ * 512:(os_ + 1) * 512],
                           xo.h[:, :], reads=[xo],
                           writes=[DST[s]] if (tt == 0 and os_ == 0) else (),
                           writes_acc=[DST[s]] if not (tt == 0 and os_ == 0) else (),
                           slot=f"st{(tt * 4 + os_) % 4}")
            drain_conv(2)
        drain_conv()
        mk.merge(XT[1], mtacc)
        mk.merge(LNT[0], QSH)

    def emit_attn_b(SRC, DST, dst_h):
        drain_conv()
        pre = set()
        emit_ln_scan(SRC, lnf_prefetch_hook(pre))
        bKB = mk.sub(BIG[0], "bKB")
        bVT = mk.sub(BIG[1], "bVT")

        def acc_mm2(pbuf, pap, lhsT_ap, rhs_ap, first, last, rbufs, inc=None):
            mk.op("t", lambda t, pap=pap, lhsT_ap=lhsT_ap, rhs_ap=rhs_ap,
                  first=first, last=last: t.matmul(
                      pap, lhsT_ap, rhs_ap, start=first, stop=last),
                  reads=rbufs,
                  writes=[pbuf] if first else (),
                  acc_final=[pbuf] if last else (),
                  inc=(last if inc is None else inc))

        for s in range(NS):
            lnf = LNF[s % 2]
            if s not in pre:
                load_lnf(lnf, s)
            for wi, nm in ((2, "attn_b_Wk"), (3, "attn_b_Wv")):
                mk.dma("s", WS[wi].h[:, 0:8192], WR[nm].h[:, :],
                       reads=[WR[nm]], writes=[WS[wi]], deps=[conv_done[nm]])
            for g in range(NKV):
                pbuf = PS[g % 4]
                for hc in range(16):
                    acc_mm2(pbuf, pbuf.h[:, :],
                            WS[2].h[:, hc * 512 + g * 128: hc * 512 + (g + 1) * 128],
                            lnf.h[:, hc * 512:(hc + 1) * 512],
                            hc == 0, hc == 15, [WS[2], lnf])
                mk.op("v", lambda v, pbuf=pbuf, g=g, s=s: v.tensor_copy(
                    bKB.h[:, g * S + s * 512: g * S + (s + 1) * 512], pbuf.h[:, :]),
                    reads=[pbuf], writes=[bKB])
            for tt in range(4):
                pbuf = PS[tt % 4]
                for hc in range(16):
                    acc_mm2(pbuf, pbuf.h[:, :],
                            lnf.h[:, hc * 512 + tt * 128: hc * 512 + (tt + 1) * 128],
                            WS[3].h[:, hc * 512:(hc + 1) * 512],
                            hc == 0, hc == 15, [lnf, WS[3]])
                mk.op("a", lambda a, pbuf=pbuf, tt=tt, s=s: a.copy(
                    bVT.h[:, (s * 4 + tt) * 512:(s * 4 + tt + 1) * 512], pbuf.h[:, :]),
                    reads=[pbuf], writes=[bVT])
            for qq in range(4):
                wq = WS[qq % 2]
                mk.dma("s", wq.h[:, 0:8192],
                       WR["attn_b_Wq"].h[qq * 128:(qq + 1) * 128, :],
                       reads=[WR["attn_b_Wq"]], writes=[wq],
                       deps=[conv_done["attn_b_Wq"]])
                med = MED[0]
                for ql in range(4):
                    pbuf = PS[ql % 4]
                    for hc in range(16):
                        acc_mm2(pbuf, pbuf.h[:, :],
                                wq.h[:, hc * 512 + ql * 128: hc * 512 + (ql + 1) * 128],
                                lnf.h[:, hc * 512:(hc + 1) * 512],
                                hc == 0, hc == 15, [wq, lnf])
                    mk.op("v", lambda v, pbuf=pbuf, ql=ql, med=med: v.tensor_copy(
                        med.h[:, ql * 512:(ql + 1) * 512], pbuf.h[:, :]),
                        reads=[pbuf], writes=[med] if ql == 0 else (),
                        writes_acc=[med] if ql != 0 else ())
                mk.dma("g", QFM.h[qq * 512:(qq + 1) * 512, s * 512:(s + 1) * 512]
                       .rearrange("(c p) o -> p c o", p=128),
                       med.h[:, 0:2048].rearrange("p (c o) -> p c o", c=4),
                       reads=[med], writes_acc=[QFM], slot="qfm")

        aout = mk.sub(LNF[0], "aout")
        pts = [mk.sub(LNT[0], f"pt{k}") for k in range(4)]
        bmask = mk.sub(SHARED4, "bmask")
        mk.dma("s", bmask.h[:, :], cst["masks"].h[:, :], reads=[cst["masks"]],
               writes=[bmask])
        pending = []
        SMB = [PS[3], PM[0], PM[3]]
        ctr = [0]
        tailbox = [None]

        def emit_normalize(h, hp, pz, aslot):
            zr = SMF[1] if hp == 0 else SMF[2]
            mk.op("v", lambda v, pz=pz, zr=zr: v.reciprocal(
                zr.h[0:1, 0:512], pz.h[0:1, 0:512]), reads=[pz], writes=[zr])
            mk.op("v", lambda v, zr=zr: v.tensor_copy(
                MSEL.h[0:1, :], zr.h[0:1, 0:512]), reads=[zr], writes=[MSEL])
            pzb = PS[2]
            mk.op("t", lambda t, pzb=pzb: t.matmul(
                pzb.h[:, :], sb_ct["ones_row"].h[:, :], MSEL.h[0:1, :],
                start=True, stop=True),
                reads=[MSEL, sb_ct["ones_row"]], writes=[pzb])
            mk.op("v", lambda v, pzb=pzb, zr=zr: v.tensor_copy(
                zr.h[:, :], pzb.h[:, :]), reads=[pzb], writes_acc=[zr])
            mk.op("g", lambda g_, aslot=aslot, zr=zr: g_.tensor_mul(
                aout.h[:, aslot * 512:(aslot + 1) * 512],
                aout.h[:, aslot * 512:(aslot + 1) * 512], zr.h[:, :]),
                reads=[zr, aout], writes_acc=[aout])

        def sweep2(i5, h):
            g = h // 4
            hp = h % 2
            nj2 = (i5 + 1) * 4
            qt = MED[hp]
            mk.dma("s", qt.h[:, 0:512],
                   QFM.h[h * 128:(h + 1) * 128, i5 * 512:(i5 + 1) * 512],
                   reads=[QFM], writes=[qt])
            pout = PM[1] if hp == 0 else PM[2]
            pz = PS[0] if hp == 0 else PS[1]
            slots = []

            def zpv(j2):
                ptb, pap = slots[j2]
                acc_mm2(pz, pz.h[0:1, 0:512], sb_ct["ones_col"].h[:, :],
                        pap, j2 == 0, j2 == nj2 - 1,
                        [ptb, sb_ct["ones_col"]])
                acc_mm2(pout, pout.h[:, :],
                        bVT.h[:, j2 * 512 + g * 128: j2 * 512 + (g + 1) * 128],
                        pap, j2 == 0, j2 == nj2 - 1, [ptb, bVT])

            for j2 in range(nj2):
                si = ctr[0]
                ctr[0] += 1
                ptb = pts[si % 4]
                pap = ptb.h[:, (si % 4) * 512:(si % 4 + 1) * 512]
                pmS = SMB[si % 3]
                mk.op("t", lambda t, pmS=pmS, j2=j2, g=g, qt=qt: t.matmul(
                    pmS.h[:, :], bKB.h[:, g * S + j2 * 128: g * S + (j2 + 1) * 128],
                    qt.h[:, 0:512], start=True, stop=True),
                    reads=[qt, bKB], writes=[pmS])
                if j2 == 0 and tailbox[0] is not None:
                    tailbox[0]()
                    tailbox[0] = None
                mk.op("a", lambda a, pmS=pmS, pap=pap: a.activation(
                    pap, pmS.h[:, :], AF.Exp, bias=CM1.h[:, 0:1]),
                    reads=[pmS, CM1], writes=[ptb])
                rel = j2 * 128 - i5 * 512
                if 0 <= rel < 512:
                    mko = (rel // 128) * 512
                    mk.op("v", lambda v, pap=pap, mko=mko: v.tensor_mul(
                        pap, pap, bmask.h[:, mko:mko + 512]),
                        reads=[ptb, bmask], writes_acc=[ptb])
                slots.append((ptb, pap))
                if j2 > 1:
                    zpv(j2 - 2)

            def tail(h=h, hp=hp, pz=pz, pout=pout, zpv=zpv, nj2=nj2):
                zpv(nj2 - 2)
                zpv(nj2 - 1)
                mk.op("v", lambda v, pout=pout, h=h: v.tensor_copy(
                    aout.h[:, h * 512:(h + 1) * 512], pout.h[:, :]),
                    reads=[pout], writes=[aout])
                pending.append((h, hp, pz, h))
            tailbox[0] = tail

        def emit_out3(i5):
                for os_ in range(4):
                    wd = WS[2 + os_ % 2]
                    mk.dma("s", wd.h[:, 0:8192],
                           WR["attn_b_Wo"].h[os_ * 128:(os_ + 1) * 128, :],
                           reads=[WR["attn_b_Wo"]], writes=[wd],
                           deps=[conv_done["attn_b_Wo"]])
                    for tsub in range(4):
                        pbuf = PS[tsub]
                        for hc in range(16):
                            acc_mm2(pbuf, pbuf.h[:, :],
                                    aout.h[:, hc * 512 + tsub * 128: hc * 512 + (tsub + 1) * 128],
                                    wd.h[:, hc * 512:(hc + 1) * 512],
                                    hc == 0, False, [aout, wd])
                        acc_mm2(pbuf, pbuf.h[:, :], sb_ct["ones_row"].h[:, :],
                                BO.h[0:1, os_ * 512:(os_ + 1) * 512], False, True,
                                [BO, sb_ct["ones_row"]])
                        xs = SMF[tsub % 2]
                        xo = SMF[2 + tsub % 2]
                        trow = i5 * 512 + tsub * 128
                        mk.dma("s", xs.h[:, :],
                               lnh_src_h(SRC)[trow:trow + 128, os_ * 512:(os_ + 1) * 512],
                               reads=[SRC[i5]], writes=[xs])
                        mk.op("v", lambda v, pbuf=pbuf, xs=xs, xo=xo: v.tensor_add(
                            xo.h[:, :], xs.h[:, :], pbuf.h[:, :]),
                            reads=[pbuf, xs], writes=[xo])
                        mk.dma("g", dst_h[trow:trow + 128, os_ * 512:(os_ + 1) * 512],
                               xo.h[:, :], reads=[xo],
                               writes=[DST[i5]] if (os_ == 0 and tsub == 0) else (),
                               writes_acc=[DST[i5]] if not (os_ == 0 and tsub == 0) else (),
                               slot=f"st{(os_ * 4 + tsub) % 4}")

        for i5 in range(NS):
            for h in range(NH):
                while len(pending) >= 1:
                    emit_normalize(*pending.pop(0))
                sweep2(i5, h)
            if tailbox[0] is not None:
                tailbox[0]()
                tailbox[0] = None
            while pending:
                emit_normalize(*pending.pop(0))
            emit_out3(i5)

        mk.merge(LNF[0], aout)
        for p in pts:
            mk.merge(LNT[0], p)

    handles = {"x": (xin_h, XIN), "out": (xout_h, XOUT),
               "ra": (resa_h, RESA), "rb": (resb_h, RESB)}
    cur_h, CUR = handles["x"]
    nsub = len(sublayers)
    for k, sl in enumerate(sublayers):
        dst_h, DSTl = handles["out"] if k == nsub - 1 else handles[("ra", "rb")[k % 2]]
        if sl == 1:
            emit_attn_a(CUR, DSTl, dst_h)
        elif sl in (2, 4):
            emit_ffn(CUR, DSTl, dst_h)
        elif sl == 3:
            emit_attn_b(CUR, DSTl, dst_h)
        cur_h, CUR = dst_h, DSTl

    fin = []
    for key in ("st0", "st1", "st2", "st3"):
        if key in mk.dma_sems:
            fin.append(("d", key, mk.dma_sems[key][1]))
    mk.wait("s", fin)

    block = es.enter_context(nc.Block())
    mk.build(block)
    es.close()
    return nc


def prep_inputs(inputs, S):
    consts = make_consts(S)
    x = np.ascontiguousarray(np.asarray(inputs["L0_attn_a_x"], np.float32))
    wnat = {nm: np.asarray(inputs["L0_" + nm], np.float32) for nm, _ in WEIGHT_SPECS}
    base = {nm + "_R": arr for nm, arr in rearrange_weights(wnat).items()}
    base["attn_b_bo"] = np.ascontiguousarray(
        np.asarray(inputs["L0_attn_b_bo"], np.float32).reshape(1, D))
    base["ffn_gate_s2d"] = np.ascontiguousarray(
        np.asarray(inputs["L0_ffn_gate_s"], np.float32).reshape(NIC, 128))
    base["ffn_up_s2d"] = np.ascontiguousarray(
        np.asarray(inputs["L0_ffn_up_s"], np.float32).reshape(NIC, 128))
    base["ffn_down_s_row"] = np.ascontiguousarray(
        np.asarray(inputs["L0_ffn_down_s"], np.float32).reshape(1, D))
    for nm, arr in consts.items():
        base[nm] = np.ascontiguousarray(arr)
    in_maps = []
    for c in range(x.shape[0]):
        m = dict(base)
        m["x"] = np.ascontiguousarray(x[c, :S, :])
        in_maps.append(m)
    return in_maps


def kernel(**inputs):
    xfull = np.asarray(inputs["L0_attn_a_x"])
    S = int(xfull.shape[1])
    ncores = int(xfull.shape[0])
    in_maps = prep_inputs(inputs, S)
    nc = build_program(S)
    res = run_bass_kernel_spmd(nc, in_maps, core_ids=list(range(ncores)))
    out = np.stack([np.asarray(r["out"], np.float32) for r in res.results], axis=0)
    return out


# version 17
# speedup vs baseline: 1.1078x; 1.1078x over previous
from contextlib import ExitStack

import numpy as np
import ml_dtypes

import concourse.bass as bass
import concourse.mybir as mybir
from concourse.bass_utils import run_bass_kernel_spmd

F32 = mybir.dt.float32
BF16 = mybir.dt.bfloat16
AF = mybir.ActivationFunctionType
ALU = mybir.AluOpType

D = 2048
CD = 1024
UP = 1024
NH = 16
DH = 128
DR = 64
NKV = 4
KVD = NKV * DH
INTER = 5632
NIC = INTER // 128
ROPE_SCALE = 40.0
EPS = 1e-5
ISQ = float(1.0 / np.sqrt(128.0))

ENGINES = ("t", "v", "a", "g", "s")
ENG_ATTR = {"t": "tensor", "v": "vector", "a": "scalar", "g": "gpsimd", "s": "sync"}


class Buf:
    __slots__ = ("h", "w", "readers", "name")

    def __init__(self, h, name):
        self.h = h
        self.name = name
        self.w = []
        self.readers = []

    def __getitem__(self, key):
        return self.h[key]


class MK:
    def __init__(self, nc, es):
        self.nc = nc
        self.es = es
        self.ops = {e: [] for e in ENGINES}
        self.esem = {e: es.enter_context(nc.semaphore(f"es_{e}")) for e in ENGINES}
        self.inc_count = {e: 0 for e in ENGINES}
        self.inc_prefix = {e: [] for e in ENGINES}
        self.dma_sems = {}
        self._nsem = 0

    def sbuf(self, name, shape, dtype):
        b = Buf(self.es.enter_context(self.nc.sbuf_tensor(name, list(shape), dtype)), name)
        self.aliases = getattr(self, "aliases", {})
        self.aliases.setdefault(id(b.h), []).append(b)
        return b

    def sub(self, parent, name):
        fence = []
        for b in self.aliases.get(id(parent.h), []):
            fence.extend(b.w)
            fence.extend(b.readers)
        nb = Buf(parent.h, name)
        nb.w = fence
        self.aliases[id(parent.h)].append(nb)
        return nb

    def merge(self, parent, sub):
        parent.w = list(parent.w) + list(sub.w)
        parent.readers = list(parent.readers) + list(sub.readers)

    def psum(self, name, shape, dtype=F32):
        return Buf(self.es.enter_context(self.nc.psum_tensor(name, list(shape), dtype)), name)

    def dram(self, name, shape, dtype, kind=None):
        if kind is None:
            h = self.nc.dram_tensor(name, list(shape), dtype)
        else:
            h = self.nc.dram_tensor(name, list(shape), dtype, kind=kind)
        return Buf(h, name)

    def dram_slabbed(self, name, shape, dtype, nslab, kind=None):
        b = self.dram(name, shape, dtype, kind)
        subs = [Buf(b.h, f"{name}_s{i}") for i in range(nslab)]
        return b.h, subs

    def _slot(self, key):
        if key not in self.dma_sems:
            self._nsem += 1
            h = self.es.enter_context(self.nc.semaphore(f"dm{self._nsem}"))
            self.dma_sems[key] = [h, 0]
        return self.dma_sems[key]

    def op(self, eng, fn, reads=(), writes=(), writes_acc=(), acc_final=(),
           deps=(), inc=True):
        dl = list(deps)
        for b in reads:
            dl.extend(b.w)
        for b in writes:
            dl.extend(b.w)
            dl.extend(b.readers)
        idx = len(self.ops[eng])
        self.ops[eng].append((dl, fn, inc))
        self.inc_count[eng] += 1 if inc else 0
        self.inc_prefix[eng].append(self.inc_count[eng])
        tok = ("e", eng, idx)
        for b in reads:
            b.readers.append(tok)
        for b in writes:
            b.w = [tok]
            b.readers = []
        for b in writes_acc:
            b.w.append(tok)
        for b in acc_final:
            b.w = [tok]
            b.readers = []
        return tok

    def dma(self, eng, out, in_, reads=(), writes=(), writes_acc=(), slot=None,
            transpose=False, deps=(), allow_slow=False):
        if slot is None:
            lst = list(writes) + list(writes_acc)
            assert len(lst) == 1
            slot = lst[0]
        key = slot if isinstance(slot, str) else id(slot)
        sem = self._slot(key)
        dl = list(deps)
        if isinstance(slot, str) and sem[1] > 0:
            dl.append(("d", key, sem[1]))
        for b in reads:
            dl.extend(b.w)
        for b in writes:
            dl.extend(b.w)
            dl.extend(b.readers)
        sem[1] += 16
        ctok = ("d", key, sem[1])

        def fn(e, out=out, in_=in_, semh=sem[0], transpose=transpose,
               allow_slow=allow_slow):
            if transpose:
                ins = e.dma_start_transpose(out, in_)
            elif allow_slow:
                ins = e.dma_start(out=out, in_=in_, allow_slow_non_contiguous=True)
            else:
                ins = e.dma_start(out=out, in_=in_)
            ins.then_inc(semh, 16)
            return ins

        self.ops[eng].append((dl, fn, False))
        self.inc_prefix[eng].append(self.inc_count[eng])
        for b in reads:
            b.readers.append(ctok)
        for b in writes:
            b.w = [ctok]
            b.readers = []
        for b in writes_acc:
            b.w.append(ctok)
        return ctok

    def wait(self, eng, toks):
        self.ops[eng].append((list(toks), lambda e: None, False))
        self.inc_prefix[eng].append(self.inc_count[eng])

    def _resolve(self, tok):
        if tok[0] == "e":
            _, eng, idx = tok
            pre = self.inc_prefix[eng]
            v = pre[idx]
            if v == (pre[idx - 1] if idx > 0 else 0):
                assert v < self.inc_count[eng], \
                    f"dep on no-inc op with no later inc ({eng} idx {idx})"
                v = v + 1
            return (self.esem[eng], v)
        _, key, v = tok
        return (self.dma_sems[key][0], v)

    def build(self, block):
        for eng_name in ENGINES:
            ops = self.ops[eng_name]
            if not ops:
                continue

            def make(eng_name=eng_name, ops=ops):
                esem = self.esem[eng_name]

                def body(e):
                    watermark = {}
                    for deps, fn, inc in ops:
                        tgt = {}
                        for tok in deps:
                            semh, v = self._resolve(tok)
                            k = id(semh)
                            if v > tgt.get(k, (None, -1))[1]:
                                tgt[k] = (semh, v)
                        for semh, v in tgt.values():
                            if v > watermark.get(id(semh), 0):
                                e.wait_ge(semh, v)
                                watermark[id(semh)] = v
                        ins = fn(e)
                        if inc:
                            ins.then_inc(esem, 1)
                return body

            getattr(block, ENG_ATTR[eng_name])(make())

    def stats(self):
        return {e: len(self.ops[e]) for e in ENGINES}


def make_consts(S):
    bfd = ml_dtypes.bfloat16
    half = DR // 2
    inv_freq = 1.0 / 10000.0 ** (np.arange(0, half, 2, dtype=np.float32) / half)
    t = np.arange(S, dtype=np.float32) / ROPE_SCALE
    freqs = np.einsum("i,j->ij", t, inv_freq)
    rot = np.concatenate([freqs, freqs], axis=1)
    cos = np.cos(rot)
    sin = np.sin(rot)
    C1 = np.ones((S, DR), np.float32)
    B1 = np.zeros((S, DR), np.float32)
    C1[:, 0:32] = cos
    B1[:, 0:16] = -sin[:, 0:16]
    B1[:, 16:32] = sin[:, 16:32]
    CT = np.tile(C1, (1, NH)).astype(bfd)
    BT = np.tile(B1, (1, NH)).astype(bfd)
    CF = np.concatenate([C1.T, C1.T], axis=0).astype(bfd)
    BFm = np.concatenate([B1.T, B1.T], axis=0).astype(bfd)
    P128 = np.zeros((128, 128), np.float32)
    for blk in range(2):
        b0 = blk * 64
        for j in range(16):
            P128[b0 + 16 + j, b0 + j] = 1.0
            P128[b0 + j, b0 + 16 + j] = 1.0
    masks = np.zeros((128, 4 * 512), np.float32)
    jj = np.arange(128)[:, None]
    ii = np.arange(512)[None, :]
    for k in range(4):
        masks[:, k * 512:(k + 1) * 512] = (ii >= jj + 128 * k)
    return {
        "rope_ct": CT, "rope_bt": BT, "rope_cf": CF, "rope_bf": BFm,
        "perm": P128.astype(bfd), "ident": np.eye(128, dtype=bfd),
        "identf": np.eye(128, dtype=np.float32),
        "ones_row": np.ones((1, 128), bfd), "ones_col": np.ones((128, 1), bfd),
        "ones4": np.ones((4, 128), bfd),
        "ones_rowf": np.ones((1, 128), np.float32),
        "masks": masks.astype(bfd),
    }


def const_specs(S):
    return [
        ("rope_ct", [S, NH * DR], BF16, False),
        ("rope_bt", [S, NH * DR], BF16, False),
        ("rope_cf", [128, S], BF16, False),
        ("rope_bf", [128, S], BF16, False),
        ("perm", [128, 128], BF16, True),
        ("ident", [128, 128], BF16, False),
        ("identf", [128, 128], F32, True),
        ("ones_row", [1, 128], BF16, True),
        ("ones_col", [128, 1], BF16, True),
        ("ones4", [4, 128], BF16, True),
        ("ones_rowf", [1, 128], F32, True),
        ("masks", [128, 4 * 512], BF16, False),
    ]


WR_SPECS = {
    "attn_a_W_dkv": (128, 16384), "attn_a_W_dq": (128, 16384),
    "attn_a_W_uk": (128, 8192), "attn_a_W_uq": (128, 8192),
    "attn_a_W_qr": (128, 8192), "attn_a_W_uv": (128, 16384),
    "attn_a_W_kr": (128, 16384), "attn_a_W_out": (128, 32768),
    "ffn_gate_w": (1408, 8192), "ffn_up_w": (1408, 8192),
    "ffn_down_w": (2048, 5632),
    "attn_b_Wq": (512, 8192), "attn_b_Wk": (128, 8192),
    "attn_b_Wv": (128, 8192), "attn_b_Wo": (512, 8192),
}


def rearrange_weights(base):
    out = {}
    g = lambda n: base[n]
    out["attn_a_W_dkv"] = g("attn_a_W_dkv").reshape(16, 128, 1024).transpose(1, 0, 2)
    out["attn_a_W_dq"] = g("attn_a_W_dq").reshape(16, 128, 1024).transpose(1, 0, 2)
    out["attn_a_W_uk"] = g("attn_a_W_uk").reshape(8, 128, 1024).transpose(1, 0, 2)
    out["attn_a_W_uq"] = g("attn_a_W_uq").reshape(8, 128, 1024).transpose(1, 0, 2)
    out["attn_a_W_qr"] = g("attn_a_W_qr").reshape(8, 128, 1024).transpose(1, 0, 2)
    out["attn_a_W_uv"] = g("attn_a_W_uv").reshape(8, 128, 2, 1024).transpose(1, 2, 0, 3)
    out["attn_a_W_kr"] = g("attn_a_W_kr").reshape(16, 128, 1024).transpose(1, 0, 2)
    out["attn_a_W_out"] = g("attn_a_W_out").reshape(16, 128, 2048).transpose(1, 0, 2)
    out["ffn_gate_w"] = g("ffn_gate_w").T.reshape(16, 128, 11, 512).transpose(2, 1, 0, 3)
    out["ffn_up_w"] = g("ffn_up_w").T.reshape(16, 128, 11, 512).transpose(2, 1, 0, 3)
    out["ffn_down_w"] = g("ffn_down_w").T.reshape(4, 11, 128, 4, 512).transpose(0, 3, 2, 1, 4)
    out["attn_b_Wq"] = g("attn_b_Wq").T.reshape(16, 128, 4, 512).transpose(2, 1, 0, 3)
    out["attn_b_Wk"] = g("attn_b_Wk").T.reshape(16, 128, 512).transpose(1, 0, 2)
    out["attn_b_Wv"] = g("attn_b_Wv").T.reshape(16, 128, 512).transpose(1, 0, 2)
    out["attn_b_Wo"] = g("attn_b_Wo").T.reshape(16, 128, 4, 512).transpose(2, 1, 0, 3)
    return {k: np.ascontiguousarray(v.reshape(WR_SPECS[k])) for k, v in out.items()}

WEIGHT_SPECS = [
    ("attn_a_W_dkv", [D, CD]), ("attn_a_W_dq", [D, CD]), ("attn_a_W_uk", [CD, UP]),
    ("attn_a_W_uv", [CD, NH * DH]), ("attn_a_W_uq", [CD, UP]),
    ("attn_a_W_kr", [D, NH * DR]), ("attn_a_W_qr", [UP, NH * DR]),
    ("attn_a_W_out", [NH * DH, D]),
    ("attn_b_Wq", [NH * DH, D]), ("attn_b_Wk", [KVD, D]), ("attn_b_Wv", [KVD, D]),
    ("attn_b_Wo", [D, NH * DH]),
    ("ffn_gate_w", [INTER, D]), ("ffn_up_w", [INTER, D]), ("ffn_down_w", [D, INTER]),
]


def build_program(S, sublayers=(1, 2, 3, 4)):
    assert S % 512 == 0
    NT = S // 128
    NS = S // 512

    nc = bass.Bass("TRN2", target_bir_lowering=False)
    es = ExitStack()
    mk = MK(nc, es)

    xin_h, XIN = mk.dram_slabbed("x", [S, D], F32, NS, kind="ExternalInput")
    win_r = {nm: mk.dram(nm + "_R", list(shape), F32, kind="ExternalInput")
             for nm, shape in WR_SPECS.items()}
    bo_in = mk.dram("attn_b_bo", [1, D], F32, kind="ExternalInput")
    gs_in = mk.dram("ffn_gate_s2d", [NIC, 128], F32, kind="ExternalInput")
    us_in = mk.dram("ffn_up_s2d", [NIC, 128], F32, kind="ExternalInput")
    cst = {nm: mk.dram(nm, shape, dt, kind="ExternalInput")
           for nm, shape, dt, _ in const_specs(S)}
    ds_row = mk.dram("ffn_down_s_row", [1, D], F32, kind="ExternalInput")
    xout_h, XOUT = mk.dram_slabbed("out", [S, D], F32, NS, kind="ExternalOutput")

    resa_h, RESA = mk.dram_slabbed("res_a", [S, D], F32, NS)
    resb_h, RESB = mk.dram_slabbed("res_b", [S, D], F32, NS)
    lnh_h, LNHS = mk.dram_slabbed("lnh", [S, D], BF16, NS)
    CKV = mk.dram("ckv", [CD, S], BF16)
    QFM = mk.dram("qfm", [NH * DH, S], BF16)
    MRW = mk.dram("mrw", [NH, 512], BF16)
    WR = {nm: mk.dram("wr_" + nm, list(WR_SPECS[nm]), BF16) for nm in WR_SPECS}

    sb_ct = {nm: mk.sbuf("c_" + nm, shape, dt)
             for nm, shape, dt, keep in const_specs(S) if keep}
    GS = mk.sbuf("gs", [128, NIC], F32)
    US = mk.sbuf("us", [128, NIC], F32)
    SHARED4 = mk.sbuf("shared4", [128, 2048], BF16)
    BO = mk.sbuf("bo", [1, D], BF16)
    XT = [mk.sbuf(f"xt{i}", [128, D], F32) for i in range(2)]
    LNT = [mk.sbuf("lnt0", [128, D], BF16)] * 2
    LNF = [mk.sbuf(f"lnf{i}", [128, 16 * 512], BF16) for i in range(2)]
    WS = [mk.sbuf(f"ws{i}", [128, 8192], BF16) for i in range(4)]
    BIG = [mk.sbuf(f"big{i}", [128, 16384], BF16) for i in range(2)]
    MED = [mk.sbuf(f"med{i}", [128, 2048], BF16) for i in range(2)]
    SMF = [mk.sbuf(f"smf{i}", [128, 512], F32) for i in range(4)]
    MSEL = mk.sbuf("msel", [1, 512], BF16)
    BN24 = [mk.sbuf(f"bn24_{i}", [128, 24], F32) for i in range(2)]
    ST1 = [mk.sbuf(f"st1_{i}", [128, 8], F32) for i in range(2)]
    CM1 = mk.sbuf("cm1", [128, 1], F32)
    CEPS = mk.sbuf("ceps", [128, 1], F32)
    PS = [mk.psum(f"ps{i}", [128, 512]) for i in range(4)]
    PM = [mk.psum(f"pm{i}", [128, 512]) for i in range(4)]

    conv_done = {}
    nconsts = sum(1 for *_, keep in const_specs(S) if keep)

    for nm in sb_ct:
        mk.dma("s", sb_ct[nm].h[:, :], cst[nm].h[:, :], reads=[cst[nm]],
               writes=[sb_ct[nm]], slot="consts")
    consts_tok = ("d", "consts", 16 * nconsts)
    mk.op("v", lambda v: v.memset(CM1.h[:, :], -20.0), writes=[CM1])
    mk.op("v", lambda v: v.memset(CEPS.h[:, :], EPS), writes=[CEPS])

    conv_order = ["attn_a_W_dkv", "attn_a_W_dq", "attn_a_W_uq", "attn_a_W_qr",
                  "attn_a_W_uv", "attn_a_W_uk", "attn_a_W_kr", "attn_a_W_out",
                  "ffn_gate_w", "ffn_up_w", "ffn_down_w",
                  "attn_b_Wq", "attn_b_Wk", "attn_b_Wv", "attn_b_Wo"]
    conv_pieces = []
    pieces_left = {}
    for nm in conv_order:
        rows, cols = WR_SPECS[nm]
        step = max(1, min(rows, (1 << 22) // (cols * 4)))
        for r0 in range(0, rows, step):
            conv_pieces.append((nm, r0, min(r0 + step, rows)))
            pieces_left[nm] = pieces_left.get(nm, 0) + 1
    cctr = [0]

    def drain_conv(n=1 << 30):
        k = 0
        while conv_pieces and k < n:
            nm, r0, r1 = conv_pieces.pop(0)
            tk = mk.dma("g", WR[nm].h[r0:r1, :], win_r[nm].h[r0:r1, :],
                        reads=[win_r[nm]], writes_acc=[WR[nm]],
                        slot=f"conv{cctr[0] % 2}")
            cctr[0] += 1
            pieces_left[nm] -= 1
            if pieces_left[nm] == 0:
                conv_done[nm] = tk
            k += 1

    drain_conv(sum(1 for nm, *_ in conv_pieces if nm.startswith("attn_a")))

    for src, dstbuf, n in ((gs_in, GS, NIC), (us_in, US, NIC)):
        tmp = SMF[0]
        mk.dma("s", tmp.h[0:n, 0:128], src.h[:, :], reads=[src], writes=[tmp])
        pt = PS[0]
        mk.op("t", lambda t, tmp=tmp, pt=pt, n=n: t.transpose(
            pt.h[:, 0:n], tmp.h[0:n, 0:128], sb_ct["identf"].h[0:n, 0:n]),
            reads=[tmp, sb_ct["identf"]], writes=[pt], deps=[consts_tok])
        mk.op("v", lambda v, pt=pt, dstbuf=dstbuf, n=n: v.tensor_copy(
            dstbuf.h[:, 0:n], pt.h[:, 0:n]), reads=[pt], writes=[dstbuf])

    mk.dma("g", BO.h[:, :], bo_in.h[:, :], reads=[bo_in], writes=[BO], slot="conv")

    def emit_ln_scan(SRC, hook=None):
        lnth = [mk.sub(LNT[0], "lnt_h0"), mk.sub(LNT[0], "lnt_h1")]
        for i in range(NT):
            sl = i // 4
            xt = XT[i % 2]
            bn = BN24[i % 2]
            st = ST1[i % 2]
            mk.dma("s", xt.h[:, :], lnh_src_h(SRC)[i * 128:(i + 1) * 128, :],
                   reads=[SRC[sl]], writes=[xt])
            for g in range(4):
                mk.op("v", lambda v, g=g, xt=xt, bn=bn: v.bn_stats(
                    bn.h[:, 6 * g:6 * g + 6], xt.h[:, 512 * g:512 * (g + 1)]),
                    reads=[xt], writes=[bn] if g == 0 else (),
                    writes_acc=[bn] if g > 0 else (), inc=(g == 3))
            mk.op("v", lambda v, bn=bn, st=st: v.bn_aggr(st.h[:, 0:2], bn.h[:, :]),
                  reads=[bn], writes=[st])
            mk.op("a", lambda a, st=st: a.activation(
                st.h[:, 3:4], st.h[:, 1:2], AF.Sqrt, bias=CEPS.h[:, 0:1]),
                reads=[st, CEPS], writes_acc=[st])
            mk.op("v", lambda v, st=st: v.reciprocal(
                st.h[:, 4:5], st.h[:, 3:4]), reads=[st], writes_acc=[st])
            mk.op("v", lambda v, st=st: v.tensor_scalar(
                st.h[:, 5:6], st.h[:, 0:1], st.h[:, 4:5], -1.0, ALU.mult, ALU.mult),
                reads=[st], writes_acc=[st])
            for hf in range(2):
                lt = lnth[hf]
                mk.op("a", lambda a, xt=xt, lt=lt, hf=hf, st=st: a.activation(
                    lt.h[:, hf * 1024:(hf + 1) * 1024],
                    xt.h[:, hf * 1024:(hf + 1) * 1024], AF.Identity,
                    bias=st.h[:, 5:6], scale=st.h[:, 4:5]),
                    reads=[xt, st], writes=[lt])
                mk.dma("g", lnh_h[i * 128:(i + 1) * 128, hf * 1024:(hf + 1) * 1024],
                       lt.h[:, hf * 1024:(hf + 1) * 1024], reads=[lt],
                       writes=[LNHS[sl]] if (i % 4 == 0 and hf == 0) else (),
                       writes_acc=[LNHS[sl]] if not (i % 4 == 0 and hf == 0) else (),
                       slot=f"lns{(2 * i + hf) % 4}")
            if i % 4 == 3 and hook is not None:
                hook(sl)

    def lnh_src_h(SRC):
        return SRC[0].h

    def lnf_prefetch_hook(pre):
        return None

    def load_lnf(dst, s):
        toks = []
        for c in range(16):
            toks.append(mk.dma("a", dst.h[:, c * 512:(c + 1) * 512],
                               lnh_h[s * 512:(s + 1) * 512, c * 128:(c + 1) * 128],
                               reads=[LNHS[s]],
                               writes=[dst] if c == 0 else (),
                               slot=dst, transpose=True))
        dst.w = [toks[-1]]
        return toks[-1]

    def emit_ffn(SRC, DST, dst_h):
        drain_conv()
        pre = set()
        emit_ln_scan(SRC, lnf_prefetch_hook(pre))
        dsbc = mk.sub(SHARED4, "dsbc")
        for q in range(4):
            mk.dma("s", SMF[0].h[0:1, 0:512], ds_row.h[0:1, q * 512:(q + 1) * 512],
                   reads=[ds_row], writes=[SMF[0]])
            pq = PS[0]
            mk.op("t", lambda t, pq=pq: t.matmul(
                pq.h[:, :], sb_ct["ones_rowf"].h[:, :], SMF[0].h[0:1, 0:512],
                start=True, stop=True), reads=[SMF[0], sb_ct["ones_rowf"]], writes=[pq])
            mk.op("v", lambda v, pq=pq, q=q: v.tensor_copy(
                dsbc.h[:, q * 512:(q + 1) * 512], pq.h[:, :]),
                reads=[pq], writes=[dsbc])
        hdn = [(BIG[0], c) if c < 32 else (BIG[1], c - 32) for c in range(NIC)]
        for s in range(NS):
            lnf = LNF[s % 2]
            if s not in pre:
                load_lnf(lnf, s)
            for grp in range(11):
                wg = WS[2 * (grp % 2)]
                wu = WS[2 * (grp % 2) + 1]
                ic0 = grp * 4
                for wbuf, nm in ((wg, "ffn_gate_w"), (wu, "ffn_up_w")):
                    mk.dma("s", wbuf.h[:, 0:8192],
                           WR[nm].h[grp * 128:(grp + 1) * 128, :],
                           reads=[WR[nm]], writes=[wbuf],
                           deps=[conv_done[nm]])
                for ici in range(4):
                    ic = ic0 + ici
                    pg = PS[2 * (ici % 2)]
                    pu = PS[2 * (ici % 2) + 1]
                    for pbuf, wbuf in ((pg, wg), (pu, wu)):
                        for hc in range(16):
                            mk.op("t", lambda t, wbuf=wbuf, pbuf=pbuf, lnf=lnf, hc=hc,
                                  ici=ici: t.matmul(
                                      pbuf.h[:, :],
                                      wbuf.h[:, hc * 512 + ici * 128: hc * 512 + (ici + 1) * 128],
                                      lnf.h[:, hc * 512:(hc + 1) * 512],
                                      start=(hc == 0), stop=(hc == 15)),
                                  reads=[wbuf, lnf],
                                  writes=[pbuf] if hc == 0 else (),
                                  acc_final=[pbuf] if hc == 15 else (),
                                  inc=(hc == 15))
                    sil = MED[ic % 2]
                    mk.op("a", lambda a, pg=pg, sil=sil, ic=ic: a.activation(
                        sil.h[:, 0:512], pg.h[:, :], AF.Silu, scale=GS.h[:, ic:ic + 1]),
                        reads=[pg, GS], writes=[sil])
                    hb, hci = hdn[ic]
                    mk.op("v", lambda v, pu=pu, sil=sil, hb=hb, hci=hci, ic=ic:
                          v.scalar_tensor_tensor(
                              hb.h[:, hci * 512:(hci + 1) * 512], pu.h[:, :],
                              US.h[:, ic:ic + 1], sil.h[:, 0:512], ALU.mult, ALU.mult),
                          reads=[pu, sil, US],
                          writes=[hb] if ic in (0, 32) else (),
                          writes_acc=[hb] if ic not in (0, 32) else ())
            for hs in range(4):
                for icg in range(4):
                    wd = WS[icg % 2]
                    mk.dma("s", wd.h[:, 0:5632],
                           WR["ffn_down_w"].h[(icg * 4 + hs) * 128:
                                              (icg * 4 + hs + 1) * 128, :],
                           reads=[WR["ffn_down_w"]], writes=[wd],
                           deps=[conv_done["ffn_down_w"]])
                    for tsub in range(4):
                        po = PM[tsub]
                        for icl in range(11):
                            ic = icg * 11 + icl
                            hb, hci = hdn[ic]
                            first = (icg == 0 and icl == 0)
                            last = (icg == 3 and icl == 10)
                            mk.op("t", lambda t, po=po, wd=wd, hb=hb, hci=hci,
                                  icl=icl, tsub=tsub, first=first, last=last: t.matmul(
                                      po.h[:, :],
                                      hb.h[:, hci * 512 + tsub * 128: hci * 512 + (tsub + 1) * 128],
                                      wd.h[:, icl * 512:(icl + 1) * 512],
                                      start=first, stop=last),
                                  reads=[wd, hb],
                                  writes=[po] if first else (),
                                  acc_final=[po] if last else (),
                                  inc=(icl == 10))
                for tsub in range(4):
                    po = PM[tsub]
                    xs = SMF[tsub % 2]
                    xo = SMF[2 + tsub % 2]
                    trow = s * 512 + tsub * 128
                    mk.dma("s", xs.h[:, :],
                           lnh_src_h(SRC)[trow:trow + 128, hs * 512:(hs + 1) * 512],
                           reads=[SRC[s]], writes=[xs])
                    mk.op("v", lambda v, po=po, xo=xo, hs=hs: v.scalar_tensor_tensor(
                        xo.h[:, :], po.h[:, :], 1.0,
                        dsbc.h[:, hs * 512:(hs + 1) * 512], ALU.mult, ALU.mult),
                        reads=[po, dsbc], writes=[xo])
                    mk.op("v", lambda v, xs=xs, xo=xo: v.tensor_add(
                        xo.h[:, :], xo.h[:, :], xs.h[:, :]),
                        reads=[xs, xo], writes_acc=[xo])
                    mk.dma("g", dst_h[trow:trow + 128, hs * 512:(hs + 1) * 512],
                           xo.h[:, :], reads=[xo],
                           writes=[DST[s]] if (hs == 0 and tsub == 0) else (),
                           writes_acc=[DST[s]] if not (hs == 0 and tsub == 0) else (),
                           slot=f"st{(hs * 4 + tsub) % 4}")

    def emit_attn_a(SRC, DST, dst_h):
        pre = set()
        emit_ln_scan(SRC, lnf_prefetch_hook(pre))
        aCKV = mk.sub(BIG[0], "aCKV")
        aCQ = mk.sub(BIG[0], "aCQ")
        aQCF = mk.sub(BIG[0], "aQCF")
        aQCT = mk.sub(BIG[0], "aQCT")
        aQRT = mk.sub(BIG[1], "aQRT")
        aQRO = mk.sub(BIG[1], "aQRO")
        aVT = mk.sub(BIG[1], "aVT")
        O = {"aCKV": 0, "aCQ": 4096, "aQCF": 8192, "aQCT": 12288,
             "aQRT": 0, "aQRO": 4096, "aVT": 8192}

        QSH = mk.sub(LNT[0], "qsha")
        mk.op("v", lambda v: v.memset(QSH.h[:, 0:1024], 0.0), writes=[QSH])
        mtacc = mk.sub(XT[1], "mtacc")
        mk.op("v", lambda v: v.memset(mtacc.h[:, :], 0.0), writes=[mtacc])

        def acc_mm(pbuf, lhsT_ap, rhs_ap, kidx, klast, rbufs, extra=()):
            mk.op("t", lambda t, lhsT_ap=lhsT_ap, rhs_ap=rhs_ap, pbuf=pbuf,
                  kidx=kidx, klast=klast: t.matmul(
                      pbuf.h[:, :], lhsT_ap, rhs_ap,
                      start=(kidx == 0), stop=(kidx == klast)),
                  reads=rbufs, deps=extra,
                  writes=[pbuf] if kidx == 0 else (),
                  acc_final=[pbuf] if kidx == klast else (),
                  inc=(kidx == klast))

        for s in range(NS):
            lnf = LNF[s % 2]
            if s not in pre:
                load_lnf(lnf, s)
            for wi, (nm, hh) in enumerate((("attn_a_W_dkv", 0), ("attn_a_W_dkv", 1),
                                           ("attn_a_W_dq", 0), ("attn_a_W_dq", 1))):
                mk.dma("s", WS[wi].h[:, 0:8192],
                       WR[nm].h[:, hh * 8192:(hh + 1) * 8192],
                       reads=[WR[nm]], writes=[WS[wi]], deps=[conv_done[nm]])
            for dstb, w01, eng in ((aCKV, (WS[0], WS[1]), "v"), (aCQ, (WS[2], WS[3]), "a")):
                for cc in range(8):
                    pbuf = PS[cc % 4]
                    for dc in range(16):
                        wb = w01[dc // 8]
                        acc_mm(pbuf, wb.h[:, (dc % 8) * 1024 + cc * 128:
                                          (dc % 8) * 1024 + (cc + 1) * 128],
                               lnf.h[:, dc * 512:(dc + 1) * 512], dc, 15, [wb, lnf])
                    off = O[dstb.name] + cc * 512
                    mk.op(eng, lambda e, pbuf=pbuf, dstb=dstb, off=off, eng=eng: (
                        e.tensor_copy if eng == "v" else e.copy)(
                        dstb.h[:, off:off + 512], pbuf.h[:, :]),
                        reads=[pbuf], writes=[dstb])
            mk.dma("g", CKV.h[:, s * 512:(s + 1) * 512].rearrange(
                "(c p) o -> p c o", p=128),
                BIG[0].h[:, 0:4096].rearrange("p (c o) -> p c o", c=8),
                reads=[aCKV], writes_acc=[CKV], slot="ckv")
            mk.dma("s", WS[0].h[:, 0:8192], WR["attn_a_W_uq"].h[:, :],
                   reads=[WR["attn_a_W_uq"]], writes=[WS[0]],
                   deps=[conv_done["attn_a_W_uq"]])
            mk.dma("s", WS[1].h[:, 0:8192], WR["attn_a_W_qr"].h[:, :],
                   reads=[WR["attn_a_W_qr"]], writes=[WS[1]],
                   deps=[conv_done["attn_a_W_qr"]])
            for wi, hh in ((2, 0), (3, 1)):
                mk.dma("s", WS[wi].h[:, 0:8192],
                       WR["attn_a_W_uv"].h[:, hh * 8192:(hh + 1) * 8192],
                       reads=[WR["attn_a_W_uv"]], writes=[WS[wi]],
                       deps=[conv_done["attn_a_W_uv"]])
            for uc in range(8):
                pbuf = PS[uc % 4]
                for cc in range(8):
                    acc_mm(pbuf, WS[0].h[:, cc * 1024 + uc * 128: cc * 1024 + (uc + 1) * 128],
                           aCQ.h[:, 4096 + cc * 512: 4096 + (cc + 1) * 512], cc, 7,
                           [WS[0], aCQ])
                mk.op("v", lambda v, pbuf=pbuf, uc=uc: v.tensor_copy(
                    aQCF.h[:, 8192 + uc * 512: 8192 + (uc + 1) * 512], pbuf.h[:, :]),
                    reads=[pbuf], writes=[aQCF])
            for tt in range(4):
                for half in range(2):
                    pbuf = PS[(tt * 2 + half) % 4]
                    for cc in range(8):
                        acc_mm(pbuf, aCQ.h[:, 4096 + cc * 512 + tt * 128:
                                           4096 + cc * 512 + (tt + 1) * 128],
                               WS[0].h[:, cc * 1024 + half * 512: cc * 1024 + (half + 1) * 512],
                               cc, 7, [aCQ, WS[0]])
                    off = 12288 + tt * 1024 + half * 512
                    mk.op("a", lambda a, pbuf=pbuf, off=off: a.copy(
                        aQCT.h[:, off:off + 512], pbuf.h[:, :]),
                        reads=[pbuf], writes=[aQCT])
            for tt in range(4):
                for half in range(2):
                    pbuf = PS[(tt * 2 + half) % 4]
                    for uc in range(8):
                        acc_mm(pbuf, aQCF.h[:, 8192 + uc * 512 + tt * 128:
                                            8192 + uc * 512 + (tt + 1) * 128],
                               WS[1].h[:, uc * 1024 + half * 512: uc * 1024 + (half + 1) * 512],
                               uc, 7, [aQCF, WS[1]])
                    off = tt * 1024 + half * 512
                    mk.op("v", lambda v, pbuf=pbuf, off=off: v.tensor_copy(
                        aQRT.h[:, off:off + 512], pbuf.h[:, :]),
                        reads=[pbuf], writes=[aQRT])
            for tt in range(4):
                med = MED[tt % 2]
                trow = s * 512 + tt * 128
                mk.dma("s", med.h[:, 0:1024], cst["rope_ct"].h[trow:trow + 128, :],
                       reads=[cst["rope_ct"]], writes=[med], slot=f"ct{tt % 2}")
                mk.dma("s", med.h[:, 1024:2048], cst["rope_bt"].h[trow:trow + 128, :],
                       reads=[cst["rope_bt"]], writes_acc=[med], slot=f"ct{tt % 2}")
                qv = aQRT.h[:, tt * 1024:(tt + 1) * 1024].rearrange(
                    "p (b c) -> p b c", b=16)
                sv_ = QSH.h[:, 0:1024].rearrange("p (b c) -> p b c", b=16)
                mk.op("v", lambda v, qv=qv, sv_=sv_: v.tensor_copy(
                    sv_[:, :, 0:16], qv[:, :, 16:32]), reads=[aQRT], writes=[QSH])
                mk.op("v", lambda v, qv=qv, sv_=sv_: v.tensor_copy(
                    sv_[:, :, 16:32], qv[:, :, 0:16]), reads=[aQRT], writes_acc=[QSH])
                mk.op("v", lambda v, tt=tt, med=med: v.tensor_mul(
                    aQRO.h[:, 4096 + tt * 1024: 4096 + (tt + 1) * 1024],
                    aQRT.h[:, tt * 1024:(tt + 1) * 1024], med.h[:, 0:1024]),
                    reads=[aQRT, med], writes=[aQRO])
                mk.op("v", lambda v, med=med: v.tensor_mul(
                    QSH.h[:, 0:1024], QSH.h[:, 0:1024], med.h[:, 1024:2048]),
                    reads=[QSH, med], writes=[QSH])
                mk.op("v", lambda v, tt=tt: v.tensor_add(
                    aQRO.h[:, 4096 + tt * 1024: 4096 + (tt + 1) * 1024],
                    aQRO.h[:, 4096 + tt * 1024: 4096 + (tt + 1) * 1024],
                    QSH.h[:, 0:1024]),
                    reads=[aQRO, QSH], writes_acc=[aQRO])
            for tt in range(4):
                for os_ in range(4):
                    pbuf = PS[(tt * 4 + os_) % 4]
                    wb = WS[2 + os_ // 2]
                    for cc in range(8):
                        acc_mm(pbuf, aCKV.h[:, cc * 512 + tt * 128: cc * 512 + (tt + 1) * 128],
                               wb.h[:, cc * 1024 + (os_ % 2) * 512:
                                    cc * 1024 + (os_ % 2 + 1) * 512], cc, 7, [aCKV, wb])
                    off = 8192 + tt * 2048 + os_ * 512
                    mk.op("a", lambda a, pbuf=pbuf, off=off: a.copy(
                        aVT.h[:, off:off + 512], pbuf.h[:, :]),
                        reads=[pbuf], writes=[aVT])
            for tt in range(4):
                for h in range(NH):
                    pm = PM[h % 4]
                    vap = aVT.h[:, 8192 + tt * 2048 + h * 128: 8192 + tt * 2048 + (h + 1) * 128]
                    mk.op("t", lambda t, pm=pm, vap=vap, tt=tt, h=h: t.matmul(
                        pm.h[:, 0:64], vap,
                        aQCT.h[:, 12288 + tt * 1024 + h * 64: 12288 + tt * 1024 + h * 64 + 64],
                        start=True, stop=False),
                        reads=[aVT, aQCT], writes=[pm], inc=False)
                    mk.op("t", lambda t, pm=pm, vap=vap, tt=tt, h=h: t.matmul(
                        pm.h[:, 64:128], vap,
                        aQRO.h[:, 4096 + tt * 1024 + h * 64: 4096 + tt * 1024 + h * 64 + 64],
                        start=False, stop=True),
                        reads=[aVT, aQRO], acc_final=[pm], inc=True)
                    mk.op("v", lambda v, pm=pm, h=h: v.tensor_add(
                        mtacc.h[:, h * 128:(h + 1) * 128],
                        mtacc.h[:, h * 128:(h + 1) * 128], pm.h[:, 0:128]),
                        reads=[pm, mtacc], writes_acc=[mtacc])
            if s >= 1:
                drain_conv(6)

        mte = mk.sub(MED[0], "mte")
        for h in range(NH):
            mk.op("v", lambda v, h=h: v.tensor_scalar_mul(
                mte.h[:, h * 128:(h + 1) * 128],
                mtacc.h[:, h * 128:(h + 1) * 128], ISQ),
                reads=[mtacc], writes=[mte])
        g01 = (mk.sub(BIG[0], "g0"), mk.sub(BIG[1], "g1"))
        for h in range(NH):
            wsb = WS[h % 2]
            mk.dma("s", wsb.h[:, 0:2048],
                   WR["attn_a_W_out"].h[:, h * 2048:(h + 1) * 2048],
                   reads=[WR["attn_a_W_out"]], writes=[wsb],
                   deps=[conv_done["attn_a_W_out"]])
            gb = g01[h // 8]
            goff = (h % 8) * 2048
            for os_ in range(4):
                pbuf = PS[os_ % 4]
                mk.op("t", lambda t, pbuf=pbuf, h=h, os_=os_, wsb=wsb: t.matmul(
                    pbuf.h[:, :], mte.h[:, h * 128:(h + 1) * 128],
                    wsb.h[:, os_ * 512:(os_ + 1) * 512], start=True, stop=True),
                    reads=[mte, wsb], writes=[pbuf])
                mk.op("a", lambda a, pbuf=pbuf, gb=gb, goff=goff, os_=os_: a.copy(
                    gb.h[:, goff + os_ * 512: goff + (os_ + 1) * 512], pbuf.h[:, :]),
                    reads=[pbuf], writes=[gb] if (h % 8 == 0 and os_ == 0) else (),
                    writes_acc=[gb] if not (h % 8 == 0 and os_ == 0) else ())

        for s in range(NS):
            lnf = LNF[s % 2]
            load_lnf(lnf, s)
            med = MED[1]
            mk.dma("s", med.h[:, 0:512], cst["rope_cf"].h[:, s * 512:(s + 1) * 512],
                   reads=[cst["rope_cf"]], writes=[med], slot="cf")
            mk.dma("s", med.h[:, 512:1024], cst["rope_bf"].h[:, s * 512:(s + 1) * 512],
                   reads=[cst["rope_bf"]], writes_acc=[med], slot="cf")
            for wi, hh in ((1, 0), (2, 1)):
                mk.dma("s", WS[wi].h[:, 0:8192],
                       WR["attn_a_W_kr"].h[:, hh * 8192:(hh + 1) * 8192],
                       reads=[WR["attn_a_W_kr"]], writes=[WS[wi]],
                       deps=[conv_done["attn_a_W_kr"]])
            for rc in range(8):
                pbuf = PS[rc % 4]
                for dc in range(16):
                    wb = WS[1 + dc // 8]
                    acc_mm(pbuf, wb.h[:, (dc % 8) * 1024 + rc * 128:
                                      (dc % 8) * 1024 + (rc + 1) * 128],
                           lnf.h[:, dc * 512:(dc + 1) * 512], dc, 15, [wb, lnf])
                mk.op("v", lambda v, pbuf=pbuf, rc=rc: v.tensor_copy(
                    WS[3].h[:, rc * 512:(rc + 1) * 512], pbuf.h[:, :]),
                    reads=[pbuf], writes=[WS[3]] if rc == 0 else (),
                    writes_acc=[WS[3]] if rc != 0 else ())
            mk.dma("s", WS[0].h[:, 0:8192], WR["attn_a_W_uk"].h[:, :],
                   reads=[WR["attn_a_W_uk"]], writes=[WS[0]],
                   deps=[conv_done["attn_a_W_uk"]])
            mk.dma("s", WS[2].h[:, 0:4096].rearrange("p (c o) -> p c o", c=8),
                   CKV.h[:, s * 512:(s + 1) * 512].rearrange("(c p) o -> p c o", p=128),
                   reads=[CKV], writes=[WS[2]])
            for uc in range(8):
                pbuf = PS[uc % 4]
                for cc in range(8):
                    acc_mm(pbuf, WS[0].h[:, cc * 1024 + uc * 128: cc * 1024 + (uc + 1) * 128],
                           WS[2].h[:, cc * 512:(cc + 1) * 512], cc, 7, [WS[0], WS[2]])
                mk.op("v", lambda v, pbuf=pbuf, uc=uc, lnf=lnf: v.tensor_copy(
                    lnf.h[0:64, 2 * uc * 512:(2 * uc + 1) * 512], pbuf.h[0:64, :]),
                    reads=[pbuf], writes=[lnf] if uc == 0 else (),
                    writes_acc=[lnf] if uc != 0 else ())
                mk.op("v", lambda v, pbuf=pbuf, uc=uc, lnf=lnf: v.tensor_copy(
                    lnf.h[0:64, (2 * uc + 1) * 512:(2 * uc + 2) * 512], pbuf.h[64:128, :]),
                    reads=[pbuf], writes_acc=[lnf])
            for rc in range(8):
                pm0 = PM[rc % 2]
                mk.op("t", lambda t, pm0=pm0, rc=rc: t.matmul(
                    pm0.h[:, :], sb_ct["perm"].h[:, :],
                    WS[3].h[:, rc * 512:(rc + 1) * 512], start=True, stop=True),
                    reads=[sb_ct["perm"], WS[3]], writes=[pm0])
                mk.op("v", lambda v, rc=rc, med=med: v.tensor_mul(
                    SMF[0].h[:, :], WS[3].h[:, rc * 512:(rc + 1) * 512],
                    med.h[:, 0:512]), reads=[WS[3], med], writes=[SMF[0]])
                mk.op("v", lambda v, pm0=pm0, med=med: v.tensor_mul(
                    SMF[1].h[:, :], pm0.h[:, :], med.h[:, 512:1024]),
                    reads=[pm0, med], writes=[SMF[1]])
                mk.op("v", lambda v, rc=rc, lnf=lnf: v.tensor_add(
                    lnf.h[64:128, 2 * rc * 512:(2 * rc + 1) * 512],
                    SMF[0].h[0:64, :], SMF[1].h[0:64, :]),
                    reads=[SMF[0], SMF[1]], writes_acc=[lnf])
                mk.op("v", lambda v, rc=rc, lnf=lnf: v.tensor_add(
                    lnf.h[64:128, (2 * rc + 1) * 512:(2 * rc + 2) * 512],
                    SMF[0].h[64:128, :], SMF[1].h[64:128, :]),
                    reads=[SMF[0], SMF[1]], writes_acc=[lnf])
            for tt in range(4):
                for os_ in range(4):
                    pbuf = PS[os_]
                    for h in range(NH):
                        gb = g01[h // 8]
                        acc_mm(pbuf, lnf.h[:, h * 512 + tt * 128: h * 512 + (tt + 1) * 128],
                               gb.h[:, (h % 8) * 2048 + os_ * 512:
                                    (h % 8) * 2048 + (os_ + 1) * 512], h, 15, [lnf, gb])
                for os_ in range(4):
                    pbuf = PS[os_]
                    xs = SMF[os_ % 2]
                    xo = SMF[2 + os_ % 2]
                    trow = s * 512 + tt * 128
                    mk.dma("s", xs.h[:, :],
                           lnh_src_h(SRC)[trow:trow + 128, os_ * 512:(os_ + 1) * 512],
                           reads=[SRC[s]], writes=[xs])
                    mk.op("v", lambda v, pbuf=pbuf, xs=xs, xo=xo: v.tensor_add(
                        xo.h[:, :], xs.h[:, :], pbuf.h[:, :]),
                        reads=[pbuf, xs], writes=[xo])
                    mk.dma("g", dst_h[trow:trow + 128, os_ * 512:(os_ + 1) * 512],
                           xo.h[:, :], reads=[xo],
                           writes=[DST[s]] if (tt == 0 and os_ == 0) else (),
                           writes_acc=[DST[s]] if not (tt == 0 and os_ == 0) else (),
                           slot=f"st{(tt * 4 + os_) % 4}")
            drain_conv(2)
        drain_conv()
        mk.merge(XT[1], mtacc)
        mk.merge(LNT[0], QSH)

    def emit_attn_b(SRC, DST, dst_h):
        drain_conv()
        pre = set()
        emit_ln_scan(SRC, lnf_prefetch_hook(pre))
        bKB = mk.sub(BIG[0], "bKB")
        bVT = mk.sub(BIG[1], "bVT")

        def acc_mm2(pbuf, pap, lhsT_ap, rhs_ap, first, last, rbufs, inc=None):
            mk.op("t", lambda t, pap=pap, lhsT_ap=lhsT_ap, rhs_ap=rhs_ap,
                  first=first, last=last: t.matmul(
                      pap, lhsT_ap, rhs_ap, start=first, stop=last),
                  reads=rbufs,
                  writes=[pbuf] if first else (),
                  acc_final=[pbuf] if last else (),
                  inc=(last if inc is None else inc))

        for s in range(NS):
            lnf = LNF[s % 2]
            if s not in pre:
                load_lnf(lnf, s)
            for wi, nm in ((2, "attn_b_Wk"), (3, "attn_b_Wv")):
                mk.dma("s", WS[wi].h[:, 0:8192], WR[nm].h[:, :],
                       reads=[WR[nm]], writes=[WS[wi]], deps=[conv_done[nm]])
            for g in range(NKV):
                pbuf = PS[g % 4]
                for hc in range(16):
                    acc_mm2(pbuf, pbuf.h[:, :],
                            WS[2].h[:, hc * 512 + g * 128: hc * 512 + (g + 1) * 128],
                            lnf.h[:, hc * 512:(hc + 1) * 512],
                            hc == 0, hc == 15, [WS[2], lnf])
                mk.op("v", lambda v, pbuf=pbuf, g=g, s=s: v.tensor_copy(
                    bKB.h[:, g * S + s * 512: g * S + (s + 1) * 512], pbuf.h[:, :]),
                    reads=[pbuf], writes=[bKB])
            for tt in range(4):
                pbuf = PS[tt % 4]
                for hc in range(16):
                    acc_mm2(pbuf, pbuf.h[:, :],
                            lnf.h[:, hc * 512 + tt * 128: hc * 512 + (tt + 1) * 128],
                            WS[3].h[:, hc * 512:(hc + 1) * 512],
                            hc == 0, hc == 15, [lnf, WS[3]])
                mk.op("a", lambda a, pbuf=pbuf, tt=tt, s=s: a.copy(
                    bVT.h[:, (s * 4 + tt) * 512:(s * 4 + tt + 1) * 512], pbuf.h[:, :]),
                    reads=[pbuf], writes=[bVT])
            for qq in range(4):
                wq = WS[qq % 2]
                mk.dma("s", wq.h[:, 0:8192],
                       WR["attn_b_Wq"].h[qq * 128:(qq + 1) * 128, :],
                       reads=[WR["attn_b_Wq"]], writes=[wq],
                       deps=[conv_done["attn_b_Wq"]])
                med = MED[0]
                for ql in range(4):
                    pbuf = PS[ql % 4]
                    for hc in range(16):
                        acc_mm2(pbuf, pbuf.h[:, :],
                                wq.h[:, hc * 512 + ql * 128: hc * 512 + (ql + 1) * 128],
                                lnf.h[:, hc * 512:(hc + 1) * 512],
                                hc == 0, hc == 15, [wq, lnf])
                    mk.op("v", lambda v, pbuf=pbuf, ql=ql, med=med: v.tensor_copy(
                        med.h[:, ql * 512:(ql + 1) * 512], pbuf.h[:, :]),
                        reads=[pbuf], writes=[med] if ql == 0 else (),
                        writes_acc=[med] if ql != 0 else ())
                mk.dma("g", QFM.h[qq * 512:(qq + 1) * 512, s * 512:(s + 1) * 512]
                       .rearrange("(c p) o -> p c o", p=128),
                       med.h[:, 0:2048].rearrange("p (c o) -> p c o", c=4),
                       reads=[med], writes_acc=[QFM], slot="qfm")

        aout = mk.sub(LNF[0], "aout")
        pts = [mk.sub(LNT[0], f"pt{k}") for k in range(4)]
        bmask = mk.sub(SHARED4, "bmask")
        mk.dma("s", bmask.h[:, :], cst["masks"].h[:, :], reads=[cst["masks"]],
               writes=[bmask])
        pending = []
        SMB = [PS[3], PM[0], PM[3]]
        ctr = [0]
        tailbox = [None]

        def emit_normalize(h, hp, pz, aslot):
            zr = SMF[1] if hp == 0 else SMF[2]
            mk.op("v", lambda v, pz=pz: v.tensor_copy(
                MSEL.h[0:1, :], pz.h[0:1, 0:512]), reads=[pz], writes=[MSEL])
            pzb = PS[2]
            mk.op("t", lambda t, pzb=pzb: t.matmul(
                pzb.h[:, :], sb_ct["ones_row"].h[:, :], MSEL.h[0:1, :],
                start=True, stop=True),
                reads=[MSEL, sb_ct["ones_row"]], writes=[pzb])
            mk.op("v", lambda v, pzb=pzb, zr=zr: v.reciprocal(
                zr.h[:, :], pzb.h[:, :]), reads=[pzb], writes=[zr])
            mk.op("g", lambda g_, aslot=aslot, zr=zr: g_.tensor_mul(
                aout.h[:, aslot * 512:(aslot + 1) * 512],
                aout.h[:, aslot * 512:(aslot + 1) * 512], zr.h[:, :]),
                reads=[zr, aout], writes_acc=[aout])

        def sweep2(i5, h):
            g = h // 4
            hp = h % 2
            nj2 = (i5 + 1) * 4
            qt = MED[hp]
            mk.dma("s", qt.h[:, 0:512],
                   QFM.h[h * 128:(h + 1) * 128, i5 * 512:(i5 + 1) * 512],
                   reads=[QFM], writes=[qt])
            pout = PM[1] if hp == 0 else PM[2]
            pz = PS[0] if hp == 0 else PS[1]
            slots = []

            def zpv(j2):
                ptb, pap = slots[j2]
                acc_mm2(pz, pz.h[0:1, 0:512], sb_ct["ones_col"].h[:, :],
                        pap, j2 == 0, j2 == nj2 - 1,
                        [ptb, sb_ct["ones_col"]])
                acc_mm2(pout, pout.h[:, :],
                        bVT.h[:, j2 * 512 + g * 128: j2 * 512 + (g + 1) * 128],
                        pap, j2 == 0, j2 == nj2 - 1, [ptb, bVT])

            for j2 in range(nj2):
                si = ctr[0]
                ctr[0] += 1
                ptb = pts[si % 4]
                pap = ptb.h[:, (si % 4) * 512:(si % 4 + 1) * 512]
                pmS = SMB[si % 3]
                mk.op("t", lambda t, pmS=pmS, j2=j2, g=g, qt=qt: t.matmul(
                    pmS.h[:, :], bKB.h[:, g * S + j2 * 128: g * S + (j2 + 1) * 128],
                    qt.h[:, 0:512], start=True, stop=True),
                    reads=[qt, bKB], writes=[pmS])
                if j2 == 0 and tailbox[0] is not None:
                    tailbox[0]()
                    tailbox[0] = None
                mk.op("a", lambda a, pmS=pmS, pap=pap: a.activation(
                    pap, pmS.h[:, :], AF.Exp, bias=CM1.h[:, 0:1]),
                    reads=[pmS, CM1], writes=[ptb])
                rel = j2 * 128 - i5 * 512
                if 0 <= rel < 512:
                    mko = (rel // 128) * 512
                    mk.op("v", lambda v, pap=pap, mko=mko: v.tensor_mul(
                        pap, pap, bmask.h[:, mko:mko + 512]),
                        reads=[ptb, bmask], writes_acc=[ptb])
                slots.append((ptb, pap))
                if j2 > 1:
                    zpv(j2 - 2)

            def tail(h=h, hp=hp, pz=pz, pout=pout, zpv=zpv, nj2=nj2):
                zpv(nj2 - 2)
                zpv(nj2 - 1)
                mk.op("v", lambda v, pout=pout, h=h: v.tensor_copy(
                    aout.h[:, h * 512:(h + 1) * 512], pout.h[:, :]),
                    reads=[pout], writes=[aout])
                pending.append((h, hp, pz, h))
            tailbox[0] = tail

        def emit_out3(i5):
                for os_ in range(4):
                    wd = WS[2 + os_ % 2]
                    mk.dma("s", wd.h[:, 0:8192],
                           WR["attn_b_Wo"].h[os_ * 128:(os_ + 1) * 128, :],
                           reads=[WR["attn_b_Wo"]], writes=[wd],
                           deps=[conv_done["attn_b_Wo"]])
                    for tsub in range(4):
                        pbuf = PS[tsub]
                        for hc in range(16):
                            acc_mm2(pbuf, pbuf.h[:, :],
                                    aout.h[:, hc * 512 + tsub * 128: hc * 512 + (tsub + 1) * 128],
                                    wd.h[:, hc * 512:(hc + 1) * 512],
                                    hc == 0, False, [aout, wd])
                        acc_mm2(pbuf, pbuf.h[:, :], sb_ct["ones_row"].h[:, :],
                                BO.h[0:1, os_ * 512:(os_ + 1) * 512], False, True,
                                [BO, sb_ct["ones_row"]])
                        xs = SMF[tsub % 2]
                        xo = SMF[2 + tsub % 2]
                        trow = i5 * 512 + tsub * 128
                        mk.dma("s", xs.h[:, :],
                               lnh_src_h(SRC)[trow:trow + 128, os_ * 512:(os_ + 1) * 512],
                               reads=[SRC[i5]], writes=[xs])
                        mk.op("v", lambda v, pbuf=pbuf, xs=xs, xo=xo: v.tensor_add(
                            xo.h[:, :], xs.h[:, :], pbuf.h[:, :]),
                            reads=[pbuf, xs], writes=[xo])
                        mk.dma("g", dst_h[trow:trow + 128, os_ * 512:(os_ + 1) * 512],
                               xo.h[:, :], reads=[xo],
                               writes=[DST[i5]] if (os_ == 0 and tsub == 0) else (),
                               writes_acc=[DST[i5]] if not (os_ == 0 and tsub == 0) else (),
                               slot=f"st{(os_ * 4 + tsub) % 4}")

        for i5 in range(NS):
            for h in range(NH):
                while len(pending) >= 1:
                    emit_normalize(*pending.pop(0))
                sweep2(i5, h)
            if tailbox[0] is not None:
                tailbox[0]()
                tailbox[0] = None
            while pending:
                emit_normalize(*pending.pop(0))
            emit_out3(i5)

        mk.merge(LNF[0], aout)
        for p in pts:
            mk.merge(LNT[0], p)

    handles = {"x": (xin_h, XIN), "out": (xout_h, XOUT),
               "ra": (resa_h, RESA), "rb": (resb_h, RESB)}
    cur_h, CUR = handles["x"]
    nsub = len(sublayers)
    for k, sl in enumerate(sublayers):
        dst_h, DSTl = handles["out"] if k == nsub - 1 else handles[("ra", "rb")[k % 2]]
        if sl == 1:
            emit_attn_a(CUR, DSTl, dst_h)
        elif sl in (2, 4):
            emit_ffn(CUR, DSTl, dst_h)
        elif sl == 3:
            emit_attn_b(CUR, DSTl, dst_h)
        cur_h, CUR = dst_h, DSTl

    fin = []
    for key in ("st0", "st1", "st2", "st3"):
        if key in mk.dma_sems:
            fin.append(("d", key, mk.dma_sems[key][1]))
    mk.wait("s", fin)

    block = es.enter_context(nc.Block())
    mk.build(block)
    es.close()
    return nc


def prep_inputs(inputs, S):
    consts = make_consts(S)
    x = np.ascontiguousarray(np.asarray(inputs["L0_attn_a_x"], np.float32))
    wnat = {nm: np.asarray(inputs["L0_" + nm], np.float32) for nm, _ in WEIGHT_SPECS}
    base = {nm + "_R": arr for nm, arr in rearrange_weights(wnat).items()}
    base["attn_b_bo"] = np.ascontiguousarray(
        np.asarray(inputs["L0_attn_b_bo"], np.float32).reshape(1, D))
    base["ffn_gate_s2d"] = np.ascontiguousarray(
        np.asarray(inputs["L0_ffn_gate_s"], np.float32).reshape(NIC, 128))
    base["ffn_up_s2d"] = np.ascontiguousarray(
        np.asarray(inputs["L0_ffn_up_s"], np.float32).reshape(NIC, 128))
    base["ffn_down_s_row"] = np.ascontiguousarray(
        np.asarray(inputs["L0_ffn_down_s"], np.float32).reshape(1, D))
    for nm, arr in consts.items():
        base[nm] = np.ascontiguousarray(arr)
    in_maps = []
    for c in range(x.shape[0]):
        m = dict(base)
        m["x"] = np.ascontiguousarray(x[c, :S, :])
        in_maps.append(m)
    return in_maps


def kernel(**inputs):
    xfull = np.asarray(inputs["L0_attn_a_x"])
    S = int(xfull.shape[1])
    ncores = int(xfull.shape[0])
    in_maps = prep_inputs(inputs, S)
    nc = build_program(S)
    res = run_bass_kernel_spmd(nc, in_maps, core_ids=list(range(ncores)))
    out = np.stack([np.asarray(r["out"], np.float32) for r in res.results], axis=0)
    return out
